# version 35
# baseline (speedup 1.0000x reference)
import numpy as np
import ml_dtypes
import concourse.bass as bass
import concourse.mybir as mybir
from concourse.bass_utils import run_bass_kernel_spmd

F32 = mybir.dt.float32
BF16 = mybir.dt.bfloat16
I32 = mybir.dt.int32
AF = mybir.ActivationFunctionType
ALU = mybir.AluOpType
AXX = mybir.AxisListType.X
BF = ml_dtypes.bfloat16

NB = 2
NT = 16
TT = 128
D = 1024
DK = 512
NCORE = 8
NI = NB * NT


def build_nc(cc_ranks=NCORE, dbg=False):
    nc = bass.Bass("TRN2", target_bir_lowering=False)

    x_h = nc.dram_tensor("x", [NB, NT, TT, D], F32, kind="ExternalInput").ap()
    wqT_h = nc.dram_tensor("wqT", [D, D], F32, kind="ExternalInput").ap()
    wkT_h = nc.dram_tensor("wkT", [D, D], F32, kind="ExternalInput").ap()
    wvT_h = nc.dram_tensor("wvT", [D, D], F32, kind="ExternalInput").ap()
    woT_h = nc.dram_tensor("woT", [D, D], F32, kind="ExternalInput").ap()
    wfT_h = nc.dram_tensor("wfT", [D, D], F32, kind="ExternalInput").ap()
    wgr_h = nc.dram_tensor("wgr", [128, D], F32, kind="ExternalInput").ap()
    wlr_h = nc.dram_tensor("wlr", [128, D], F32, kind="ExternalInput").ap()
    bgr_h = nc.dram_tensor("bgr", [128, D], F32, kind="ExternalInput").ap()
    blr_h = nc.dram_tensor("blr", [128, D], F32, kind="ExternalInput").ap()
    bias_h = nc.dram_tensor("biasr", [128, D], F32, kind="ExternalInput").ap()
    tri_h = nc.dram_tensor("tri", [128, 128], BF16, kind="ExternalInput").ap()
    idn_h = nc.dram_tensor("idn", [128, 128], BF16, kind="ExternalInput").ap()
    sel_h = nc.dram_tensor("sel127", [128, 128], BF16, kind="ExternalInput").ap()
    idf_h = nc.dram_tensor("idnf", [128, 128], F32, kind="ExternalInput").ap()

    xr_h = nc.dram_tensor("xr_spill", [NB, NT, TT, D], F32, kind="Internal").ap()
    cci_h = nc.dram_tensor("cc_i", [1, 8], F32, kind="Internal").ap()
    cco_h = nc.dram_tensor("cc_o", [1, 8], F32, kind="Internal").ap()
    out_h = nc.dram_tensor("out", [NB, NT, TT, D], F32, kind="ExternalOutput").ap()
    dbg_h = (nc.dram_tensor("dbg", [NB, NT, TT, D], F32, kind="ExternalOutput").ap()
             if dbg else None)
    dbg2_h = (nc.dram_tensor("dbg2", [NI, TT, D], BF16, kind="ExternalOutput").ap()
              if dbg else None)
    dbg3_h = (nc.dram_tensor("dbg3", [NI, TT, D], BF16, kind="ExternalOutput").ap()
              if dbg else None)
    dbg4_h = (nc.dram_tensor("dbg4", [NI, TT, D], BF16, kind="ExternalOutput").ap()
              if dbg else None)

    import contextlib
    es = contextlib.ExitStack()
    wq_s = es.enter_context(nc.sbuf_tensor("wq_s", [128, 8 * D], BF16))
    wk_s = es.enter_context(nc.sbuf_tensor("wk_s", [128, 8 * D], BF16))
    wv_s = es.enter_context(nc.sbuf_tensor("wv_s", [128, 8 * D], BF16))
    wo_s = es.enter_context(nc.sbuf_tensor("wo_s", [128, 8 * D], BF16))
    wgr_s = es.enter_context(nc.sbuf_tensor("wgr_s", [128, D], BF16))
    wlr_s = es.enter_context(nc.sbuf_tensor("wlr_s", [128, D], BF16))
    bgr_s = es.enter_context(nc.sbuf_tensor("bgr_s", [128, D], F32))
    blr_s = es.enter_context(nc.sbuf_tensor("blr_s", [128, D], F32))
    bias_s = es.enter_context(nc.sbuf_tensor("bias_s", [128, D], F32))
    tri_s = es.enter_context(nc.sbuf_tensor("tri_s", [128, 128], BF16))
    idn_s = es.enter_context(nc.sbuf_tensor("idn_s", [128, 128], BF16))
    sel_s = es.enter_context(nc.sbuf_tensor("sel_s", [128, 128], BF16))
    idf_s = es.enter_context(nc.sbuf_tensor("idf_s", [128, 128], F32))
    gbc_s = es.enter_context(nc.sbuf_tensor("gbc_s", [128, 8], F32))
    S0_s = es.enter_context(nc.sbuf_tensor("S0_s", [128, 4 * DK], BF16))
    S1_s = es.enter_context(nc.sbuf_tensor("S1_s", [128, 4 * DK], BF16))
    xb_s = es.enter_context(nc.sbuf_tensor("xb_s", [128, 6 * D], F32))
    x1_s = es.enter_context(nc.sbuf_tensor("x1_s", [128, 4 * D], BF16))
    x1T_s = es.enter_context(nc.sbuf_tensor("x1T_s", [128, D], BF16))
    Kb_s = es.enter_context(nc.sbuf_tensor("Kb_s", [128, 2 * D], BF16))
    Vb_s = es.enter_context(nc.sbuf_tensor("Vb_s", [128, 4 * D], BF16))
    QT_s = es.enter_context(nc.sbuf_tensor("QT_s", [128, 4 * D], BF16))
    VT_s = es.enter_context(nc.sbuf_tensor("VT_s", [128, 4 * D], BF16))
    kn_s = es.enter_context(nc.sbuf_tensor("kn_s", [128, D], BF16))
    ktl_s = es.enter_context(nc.sbuf_tensor("ktl_s", [128, 4 * D], BF16))
    kht_s = es.enter_context(nc.sbuf_tensor("kht_s", [128, D], BF16))
    logr_s = es.enter_context(nc.sbuf_tensor("logr_s", [128, 2 * D], BF16))
    lam_s = es.enter_context(nc.sbuf_tensor("lam_s", [128, D], BF16))
    e2_s = es.enter_context(nc.sbuf_tensor("e2_s", [128, D], BF16))
    aarg_s = es.enter_context(nc.sbuf_tensor("aarg_s", [128, D], F32))
    larg_s = es.enter_context(nc.sbuf_tensor("larg_s", [128, D], F32))
    wb_s = es.enter_context(nc.sbuf_tensor("wb_s", [128, 2 * D], F32))
    ab_s = es.enter_context(nc.sbuf_tensor("ab_s", [128, 256], BF16))
    lcr_s = es.enter_context(nc.sbuf_tensor("lcr_s", [128, D], BF16))
    yb_s = es.enter_context(nc.sbuf_tensor("yb_s", [128, D], BF16))
    yt_s = es.enter_context(nc.sbuf_tensor("yt_s", [128, D], BF16))
    xr_s = es.enter_context(nc.sbuf_tensor("xr_s", [128, 2 * D], F32))
    vsc_s = es.enter_context(nc.sbuf_tensor("vsc_s", [128, D], BF16))
    dmy_s = es.enter_context(nc.sbuf_tensor("dmy_s", [128, D], BF16))
    wf_s = wq_s
    st_s = es.enter_context(nc.sbuf_tensor("st_s", [128, 128], F32))
    x2n_s = Kb_s[:].bitcast(F32)[:, 0:D]
    o1_s = QT_s[:].bitcast(F32)[:, 0:D]
    qin_s = Vb_s[:, 0:D]
    qT_s = Vb_s[:, D:2 * D]
    t8_s = es.enter_context(nc.sbuf_tensor("t8_s", [1, 8], F32))
    g8_s = es.enter_context(nc.sbuf_tensor("g8_s", [1, 8], F32))
    pK = es.enter_context(nc.psum_tensor("pK", [128, D], F32))
    pV = es.enter_context(nc.psum_tensor("pV", [128, D], F32))
    pQ = es.enter_context(nc.psum_tensor("pQ", [128, D], F32))
    pM = es.enter_context(nc.psum_tensor("pM", [128, D], F32))
    sT = es.enter_context(nc.semaphore("sT"))
    sV = es.enter_context(nc.semaphore("sV"))
    sS = es.enter_context(nc.semaphore("sS"))
    sG = es.enter_context(nc.semaphore("sG"))
    sX = es.enter_context(nc.semaphore("sX"))
    sW = es.enter_context(nc.semaphore("sW"))
    sR = es.enter_context(nc.semaphore("sR"))
    sL = es.enter_context(nc.semaphore("sL"))
    sO = es.enter_context(nc.semaphore("sO"))
    sC = es.enter_context(nc.semaphore("sC"))
    sF = es.enter_context(nc.semaphore("sF"))
    sWg = es.enter_context(nc.semaphore("sWg"))
    sX1 = es.enter_context(nc.semaphore("sX1"))
    sL1 = es.enter_context(nc.semaphore("sL1"))
    sF1 = es.enter_context(nc.semaphore("sF1"))
    sX2 = es.enter_context(nc.semaphore("sX2"))
    sX3 = es.enter_context(nc.semaphore("sX3"))
    sX4 = es.enter_context(nc.semaphore("sX4"))
    sX5 = es.enter_context(nc.semaphore("sX5"))
    sD = es.enter_context(nc.semaphore("sD"))
    sR1 = es.enter_context(nc.semaphore("sR1"))
    sCg = es.enter_context(nc.semaphore("sCg"))
    blk = es.enter_context(nc.Block())
    if True:
        C_SUM, C_SSQ, C_M, C_RS, C_T0, C_T1, C_RMAX, C_RMIN = range(8)
        C_VMAX = 8
        C_WABS = 9
        C_SC = 10
        C_SSQH = 16
        C_RNH = 18
        C_SUM2, C_SSQ2 = 14, 15
        C_E5, C_E24, C_E38 = 20, 21, 22
        C_M2 = 32
        C_RS2 = 64

        sems = {"T": sT, "V": sV, "S": sS, "G": sG}
        dsem = {"sX": sX, "sW": sW, "sR": sR, "sL": sL, "sO": sO,
                "sC": sC, "sF": sF, "sWg": sWg, "sCg": sCg,
                "sX1": sX1, "sL1": sL1, "sF1": sF1,
                "sX0": sX, "sX2": sX2, "sX3": sX3, "sD": sD, "sR1": sR1,
                "sX4": sX4, "sX5": sX5}
        cnt = {"T": 0, "V": 0, "S": 0, "G": 0}
        dcnt = {k: 0 for k in dsem}
        marks = {}

        def program(me, eng):
            for k in cnt:
                cnt[k] = 0
            for k in dcnt:
                dcnt[k] = 0
            marks.clear()

            def _w(waits):
                for sn, val in waits:
                    if val > 0:
                        eng.wait_ge(sems.get(sn) or dsem[sn], val)

            def op(tag, fn, waits=(), mark=None, drain=True):
                if me == tag:
                    _w(waits)
                    inst = fn(eng)
                    if mark is not None:
                        inst.then_inc(sems[tag], 1)
                    if drain and tag in ("V", "S", "G"):
                        eng.drain()
                if mark is not None:
                    cnt[tag] += 1
                    marks[mark] = cnt[tag]

            def dma(tag, sname, fn, waits=()):
                if me == tag:
                    _w(waits)
                    fn(eng).then_inc(dsem[sname], 16)
                dcnt[sname] += 16

            def wait_only(tag, waits):
                if me == tag:
                    _w(waits)

            T = lambda *a, **k: op("T", *a, **k)
            V = lambda *a, **k: op("V", *a, **k)
            S = lambda *a, **k: op("S", *a, **k)
            G = lambda *a, **k: op("G", *a, **k)
            M = marks

            pMb = pM[:].bitcast(BF16)
            i1ap = aarg_s[:].bitcast(I32)

            for dst, src in [(wq_s, wqT_h), (wk_s, wkT_h), (wv_s, wvT_h),
                             (wo_s, woT_h)]:
                for et in range(8):
                    dma("G", "sWg", lambda e, dst=dst, src=src, et=et:
                        e.dma_start(dst[:, et * D:(et + 1) * D],
                                    src[et * 128:(et + 1) * 128, :]))
            for dst, src in [(wgr_s, wgr_h), (wlr_s, wlr_h)]:
                dma("G", "sWg", lambda e, dst=dst, src=src: e.dma_start(dst[:], src))
            for dst, src in [(bgr_s, bgr_h), (blr_s, blr_h), (bias_s, bias_h),
                             (tri_s, tri_h), (idn_s, idn_h), (sel_s, sel_h),
                             (idf_s, idf_h)]:
                dma("Y", "sW", lambda e, dst=dst, src=src: e.dma_start(dst[:], src))
            W_ALL = dcnt["sW"]
            WG_ALL = dcnt["sWg"]
            V(lambda e: e.memset(st_s[:, C_VMAX:C_VMAX + 1], 0.0))
            V(lambda e: e.memset(st_s[:, C_E5:C_E5 + 1], 1e-5))
            V(lambda e: e.memset(st_s[:, C_E24:C_E24 + 1], 1e-24))
            V(lambda e: e.memset(st_s[:, C_E38:C_E38 + 1], 1e-38))
            V(lambda e: e.memset(st_s[:, C_WABS:C_WABS + 1], 0.0),
              mark=("init", 0))

            C_SSQH2 = C_SSQH
            C_RNH4 = 24
            C_U0, C_U1 = 96, 97

            def sl(i):
                return slice((i % 4) * D, (i % 4 + 1) * D)

            def sl2(i):
                return slice((i % 2) * D, (i % 2 + 1) * D)

            def stageA0(i):
                b, tc = i // NT, i % NT
                xbuf = xb_s[:, (i % 6) * D:(i % 6 + 1) * D]
                xsem = "sX%d" % (i % 6)
                x1buf = x1_s[:, sl(i)]
                dma("Y", xsem,
                    lambda e, b=b, tc=tc, xbuf=xbuf: e.dma_start(
                        xbuf, x_h[b, tc]),
                    waits=([("V", M[("xfree", i - 6)])] if i >= 6 else []) +
                          ([("V", M[("wabs", i - 28)])] if i >= 30 else []))
                V(lambda e, xbuf=xbuf: e.reduce_sum(
                    st_s[:, C_SUM:C_SUM + 1], xbuf, AXX),
                  waits=[(xsem, 16 * (i // 6 + 1))] +
                        ([("V", M[("x1", i - 1)])] if i >= 1 else []))
                S(lambda e, xbuf=xbuf: e.activation(
                    dmy_s[:], xbuf, AF.Square,
                    accum_out=st_s[:, C_SSQ:C_SSQ + 1]),
                  waits=[(xsem, 16 * (i // 6 + 1)), ("sW", W_ALL),
                         ("sWg", WG_ALL)] +
                        ([("V", M[("x1", i - 1)])] if i >= 1 else []),
                  mark=("ln_stats", i), drain=False)
                V(lambda e: e.tensor_scalar(
                    st_s[:, C_M:C_M + 1], st_s[:, C_SUM:C_SUM + 1],
                    1.0 / D, None, ALU.mult))
                V(lambda e: e.tensor_scalar(
                    st_s[:, C_T0:C_T0 + 1], st_s[:, C_SSQ:C_SSQ + 1],
                    1.0 / D, None, ALU.mult),
                  waits=[("S", M[("ln_stats", i)])], drain=False)
                V(lambda e: e.tensor_tensor(
                    st_s[:, C_T1:C_T1 + 1], st_s[:, C_M:C_M + 1],
                    st_s[:, C_M:C_M + 1], ALU.mult))
                V(lambda e: e.tensor_sub(
                    st_s[:, C_T0:C_T0 + 1], st_s[:, C_T0:C_T0 + 1],
                    st_s[:, C_T1:C_T1 + 1]), mark=("ln_var", i),
                  drain=False)
                S(lambda e: e.activation(
                    st_s[:, C_RS:C_RS + 1], st_s[:, C_T0:C_T0 + 1],
                    AF.Ln, bias=st_s[:, C_E5:C_E5 + 1]),
                  waits=[("V", M[("ln_var", i)])])
                S(lambda e: e.activation(
                    st_s[:, C_RS:C_RS + 1], st_s[:, C_RS:C_RS + 1],
                    AF.Exp, scale=-0.5), mark=("ln_rs", i), drain=False)
                V(lambda e, xbuf=xbuf, x1buf=x1buf: e.tensor_scalar(
                    x1buf, xbuf, st_s[:, C_M:C_M + 1],
                    st_s[:, C_RS:C_RS + 1], ALU.subtract, ALU.mult),
                  waits=[("S", M[("ln_rs", i)])] +
                        ([("T", M[("x1T_t", i - 4)])] if i >= 4 else []),
                  mark=("x1", i), drain=False)

            def pc_x1tr(i):
                b, tc = i // NT, i % NT
                xbuf = xb_s[:, (i % 6) * D:(i % 6 + 1) * D]
                x1buf = x1_s[:, sl(i)]
                pQb = pQ[:].bitcast(BF16)
                pMb = pM[:].bitcast(BF16)
                xrb = xr_s[:, sl2(i)]
                rsem = "sR" if i % 2 == 0 else "sR1"
                pQb = pQ[:].bitcast(BF16)
                for et in range(8):
                    T(lambda e, et=et, i=i: e.transpose(
                        pQb[:, et * 128:(et + 1) * 128],
                        x1_s[:, (i % 4) * D + et * 128:
                             (i % 4) * D + (et + 1) * 128], idn_s[:]),
                      waits=([("V", M[("x1", i)])] +
                             ([("S", M[("VTevac", i - 1)])] if i >= 1 else []))
                      if et == 0 else (),
                      mark=("x1T_t", i) if et == 7 else None)
                S(lambda e: e.activation(x1T_s[:], pQb[:, 0:D], AF.Copy),
                  waits=[("T", M[("x1T_t", i)])], mark=("x1T", i),
                  drain=False)
            def pc_kv(i):
                b, tc = i // NT, i % NT
                xbuf = xb_s[:, (i % 6) * D:(i % 6 + 1) * D]
                x1buf = x1_s[:, sl(i)]
                pQb = pQ[:].bitcast(BF16)
                pMb = pM[:].bitcast(BF16)
                xrb = xr_s[:, sl2(i)]
                rsem = "sR" if i % 2 == 0 else "sR1"
                for et in range(8):
                    for half in range(2):
                        T(lambda e, et=et, half=half: e.matmul(
                            pK[:, half * DK:(half + 1) * DK],
                            x1T_s[:, et * 128:(et + 1) * 128],
                            wk_s[:, et * D + half * DK:
                                 et * D + (half + 1) * DK],
                            start=(et == 0), stop=(et == 7)),
                          waits=([("S", M[("x1T", i)])] +
                                 ([("S", M[("Kevac", i - 1)]),
                                   ("S", M[("Vevac", i - 1)])]
                                  if i >= 1 else []) +
                                 ([("V", M[("SmulH", 1, 1, i - 3)])]
                                  if i - 3 >= 0 else []))
                          if (et == 0 and half == 0) else ())
                        T(lambda e, et=et, half=half: e.matmul(
                            pV[:, half * DK:(half + 1) * DK],
                            x1T_s[:, et * 128:(et + 1) * 128],
                            wv_s[:, et * D + half * DK:
                                 et * D + (half + 1) * DK],
                            start=(et == 0), stop=(et == 7)),
                          mark=("KVmm", i)
                          if (et == 7 and half == 1) else None)
                S(lambda e: e.activation(Kb_s[:, sl2(i)], pK[:], AF.Copy),
                  waits=[("T", M[("KVmm", i)])], mark=("Kevac", i),
                  drain=False)
                S(lambda e: e.activation(Vb_s[:, sl(i)], pV[:], AF.Copy),
                  waits=[("T", M[("KVmm", i)])], mark=("Vevac", i),
                  drain=False)

            def pc_qt(i):
                b, tc = i // NT, i % NT
                xbuf = xb_s[:, (i % 6) * D:(i % 6 + 1) * D]
                x1buf = x1_s[:, sl(i)]
                pQb = pQ[:].bitcast(BF16)
                pMb = pM[:].bitcast(BF16)
                xrb = xr_s[:, sl2(i)]
                rsem = "sR" if i % 2 == 0 else "sR1"
                for dt in range(8):
                    for et in range(8):
                        T(lambda e, dt=dt, et=et: e.matmul(
                            pQ[:, dt * 128:(dt + 1) * 128],
                            wq_s[:, et * D + dt * 128:
                                 et * D + (dt + 1) * 128],
                            x1T_s[:, et * 128:(et + 1) * 128],
                            start=(et == 0), stop=(et == 7)),
                          waits=([("S", M[("x1T", i)])] +
                                 ([("S", M[("lcr", i - 2)])] if i >= 2
                                  else []))
                          if (dt == 0 and et == 0) else (),
                          mark=("QTmm", i)
                          if (dt == 7 and et == 7) else None)
                S(lambda e: e.activation(QT_s[:, sl(i)], pQ[:], AF.Copy),
                  waits=[("T", M[("QTmm", i)])], mark=("QTevac", i),
                  drain=False)

            def pc_vt(i):
                b, tc = i // NT, i % NT
                xbuf = xb_s[:, (i % 6) * D:(i % 6 + 1) * D]
                x1buf = x1_s[:, sl(i)]
                pQb = pQ[:].bitcast(BF16)
                pMb = pM[:].bitcast(BF16)
                xrb = xr_s[:, sl2(i)]
                rsem = "sR" if i % 2 == 0 else "sR1"
                for j in range(8):
                    T(lambda e, j=j: e.transpose(
                        pQb[:, j * 128:(j + 1) * 128],
                        Vb_s[:, (i % 4) * D + j * 128:
                             (i % 4) * D + (j + 1) * 128], idn_s[:]),
                      waits=[("S", M[("Vevac", i)]),
                             ("S", M[("QTevac", i)])] if j == 0 else (),
                      mark=("VT_t", i) if j == 7 else None)
                S(lambda e: e.activation(VT_s[:, sl(i)], pQb[:, 0:D],
                                         AF.Copy),
                  waits=[("T", M[("VT_t", i)])], mark=("VTevac", i),
                  drain=False)
            def pc_norm(i):
                b, tc = i // NT, i % NT
                xbuf = xb_s[:, (i % 6) * D:(i % 6 + 1) * D]
                x1buf = x1_s[:, sl(i)]
                pQb = pQ[:].bitcast(BF16)
                pMb = pM[:].bitcast(BF16)
                xrb = xr_s[:, sl2(i)]
                rsem = "sR" if i % 2 == 0 else "sR1"
                for h in range(2):
                    S(lambda e, h=h: e.activation(
                        dmy_s[:, h * DK:(h + 1) * DK],
                        Kb_s[:, (i % 2) * D + h * DK:
                             (i % 2) * D + (h + 1) * DK],
                        AF.Square,
                        accum_out=st_s[:, C_SSQH2 + h:C_SSQH2 + h + 1]),
                      waits=[("S", M[("Kevac", i)])] if h == 0 else (),
                      drain=(h == 1))
                S(lambda e: e.activation(
                    st_s[:, C_RNH4 + (i % 4) * 2:C_RNH4 + (i % 4) * 2 + 2],
                    st_s[:, C_SSQH2:C_SSQH2 + 2],
                    AF.Ln, bias=st_s[:, C_E24:C_E24 + 1]))
                S(lambda e: e.activation(
                    st_s[:, C_RNH4 + (i % 4) * 2:C_RNH4 + (i % 4) * 2 + 2],
                    st_s[:, C_RNH4 + (i % 4) * 2:C_RNH4 + (i % 4) * 2 + 2],
                    AF.Exp, scale=-0.5), mark=("rnorm", i), drain=False)

            def stageBp(i):
                for h in range(2):
                    V(lambda e, h=h: e.tensor_scalar(
                        kn_s[:, h * DK:(h + 1) * DK],
                        Kb_s[:, (i % 2) * D + h * DK:
                             (i % 2) * D + (h + 1) * DK],
                        st_s[:, C_RNH4 + (i % 4) * 2 + h:
                             C_RNH4 + (i % 4) * 2 + h + 1], None, ALU.mult),
                      waits=([("S", M[("rnorm", i)])] +
                             ([("V", M[("ktil", i - 1)])] if i >= 1 else []))
                      if h == 0 else (),
                      drain=(h == 1))
                V(lambda e: e.tensor_mul(aarg_s[:], kn_s[:], wgr_s[:]),
                  drain=False)
                V(lambda e: e.tensor_mul(larg_s[:], kn_s[:], wlr_s[:]))
                V(lambda e: e.tensor_add(aarg_s[:], aarg_s[:], bgr_s[:]),
                  drain=False)
                V(lambda e: e.tensor_add(larg_s[:], larg_s[:], blr_s[:]),
                  mark=("args", i), drain=False)
                S(lambda e: e.activation(aarg_s[:], aarg_s[:], AF.Sigmoid),
                  waits=[("V", M[("args", i)])], drain=False)
                S(lambda e: e.activation(larg_s[:], larg_s[:], AF.Sigmoid),
                  mark=("sig", i), drain=False)
                V(lambda e, i=i: e.scalar_tensor_tensor(
                    wb_s[:, sl2(i)], larg_s[:], 1.0, aarg_s[:],
                    ALU.subtract, ALU.mult),
                  waits=[("S", M[("sig", i)])] +
                        ([("S", M[("logr", i - 2)])] if i >= 2 else []),
                  mark=("w", i))
                V(lambda e, i=i: e.scalar_tensor_tensor(
                    ktl_s[:, sl(i)], wb_s[:, sl2(i)], -1.0, kn_s[:],
                    ALU.subtract, ALU.mult),
                  waits=([("V", M[("khat", i - 4)])] if i >= 4 else []),
                  mark=("ktil", i), drain=False)

            def stageBpost(i):
                S(lambda e, i=i: e.activation(
                    logr_s[:, sl2(i)], wb_s[:, sl2(i)], AF.Ln,
                    scale=-1.0, bias=st_s[:, C_E38:C_E38 + 1]),
                  waits=[("V", M[("w", i)])] +
                        ([("T", M[("Lmm", i - 2)])] if i >= 2 else []),
                  mark=("logr", i), drain=False)

            def pc_bm_pre(i):
                b, tc = i // NT, i % NT
                xbuf = xb_s[:, (i % 6) * D:(i % 6 + 1) * D]
                x1buf = x1_s[:, sl(i)]
                pQb = pQ[:].bitcast(BF16)
                pMb = pM[:].bitcast(BF16)
                xrb = xr_s[:, sl2(i)]
                rsem = "sR" if i % 2 == 0 else "sR1"
                for half in range(2):
                    T(lambda e, half=half, i=i: e.matmul(
                        pM[:, half * DK:(half + 1) * DK], tri_s[:],
                        logr_s[:, (i % 2) * D + half * DK:
                               (i % 2) * D + (half + 1) * DK],
                        start=True, stop=True),
                      waits=([("S", M[("logr", i)])] +
                             ([("V", M[("xfree", i - 1)])]
                              if i >= 1 else []))
                      if half == 0 else (),
                      mark=("Lmm", i) if half == 1 else None)
                S(lambda e: e.activation(lam_s[:], pM[:], AF.Exp),
                  waits=[("T", M[("Lmm", i)])], drain=False)
                S(lambda e: e.activation(e2_s[:], pM[:], AF.Exp,
                                         scale=-1.0), mark=("Lexps", i),
                  drain=False)
                V(lambda e, i=i: e.tensor_mul(kht_s[:], ktl_s[:, sl(i)],
                                              e2_s[:]),
                  waits=[("S", M[("Lexps", i)]), ("V", M[("ktil", i)])] +
                        ([("sD", 48 * i)] if (dbg_h is not None and i >= 1)
                         else []),
                  mark=("khat", i), drain=False)
            def pc_bm_lc(i):
                b, tc = i // NT, i % NT
                xbuf = xb_s[:, (i % 6) * D:(i % 6 + 1) * D]
                x1buf = x1_s[:, sl(i)]
                pQb = pQ[:].bitcast(BF16)
                pMb = pM[:].bitcast(BF16)
                xrb = xr_s[:, sl2(i)]
                rsem = "sR" if i % 2 == 0 else "sR1"
                for h in range(2):
                    T(lambda e, h=h, i=i: e.matmul(
                        pQ[:, h * DK:(h + 1) * DK], sel_s[:],
                        logr_s[:, (i % 2) * D + h * DK:
                               (i % 2) * D + (h + 1) * DK],
                        start=True, stop=True),
                      waits=[("S", M[("logr", i)]),
                             (("S", M[("x1T", i + 2)]) if i + 2 < NI else
                              ("S", M[("VTevac", NI - 1)]))]
                      if h == 0 else (),
                      mark=("lcmm", i) if h == 1 else None)
                S(lambda e: e.activation(lcr_s[:], pQ[:], AF.Exp),
                  waits=[("T", M[("lcmm", i)])] +
                        ([("V", M[("SmulH", 1, 1, i - 1)])]
                         if i >= 1 else []),
                  mark=("lcr", i), drain=False)
            def pc_bm_A(i):
                b, tc = i // NT, i % NT
                xbuf = xb_s[:, (i % 6) * D:(i % 6 + 1) * D]
                x1buf = x1_s[:, sl(i)]
                pQb = pQ[:].bitcast(BF16)
                pMb = pM[:].bitcast(BF16)
                xrb = xr_s[:, sl2(i)]
                rsem = "sR" if i % 2 == 0 else "sR1"
                for h in range(2):
                    for ct in range(4):
                        T(lambda e, h=h, ct=ct: e.matmul(
                            pM[:, h * DK: h * DK + 128],
                            VT_s[:, (i % 4) * D + (h * 4 + ct) * 128:
                                 (i % 4) * D + (h * 4 + ct + 1) * 128],
                            QT_s[:, (i % 4) * D + (h * 4 + ct) * 128:
                                 (i % 4) * D + (h * 4 + ct + 1) * 128],
                            start=(ct == 0), stop=(ct == 3)),
                          waits=[("S", M[("lcr", i)]),
                                 ("S", M[("Lexps", i)])]
                          if (h == 0 and ct == 0) else (),
                          mark=("Amm%d" % h, i) if ct == 3 else None)
                    V(lambda e, h=h: e.tensor_tensor(
                        ab_s[:, h * 128:(h + 1) * 128],
                        pM[:, h * DK: h * DK + 128], tri_s[:], ALU.mult),
                      waits=[("T", M[("Amm%d" % h, i)])],
                      mark=("Aevac%d" % h, i), drain=False)
            def pc_bm_Y(i):
                b, tc = i // NT, i % NT
                xbuf = xb_s[:, (i % 6) * D:(i % 6 + 1) * D]
                x1buf = x1_s[:, sl(i)]
                pQb = pQ[:].bitcast(BF16)
                pMb = pM[:].bitcast(BF16)
                xrb = xr_s[:, sl2(i)]
                rsem = "sR" if i % 2 == 0 else "sR1"
                for h in range(2):
                    Sh = (S0_s, S1_s)[h]
                    first = True
                    if tc > 0:
                        for vt in range(4):
                            T(lambda e, h=h, vt=vt, Sh=Sh: e.matmul(
                                pM[:, h * DK:(h + 1) * DK],
                                QT_s[:, (i % 4) * D + (h * 4 + vt) * 128:
                                     (i % 4) * D + (h * 4 + vt + 1) * 128],
                                Sh[:, vt * DK:(vt + 1) * DK],
                                start=(vt == 0), stop=False),
                              waits=[("V", M[("Aevac%d" % h, i)]),
                                     ("V", M[("SmulH", 1, 1, i - 1)])]
                              if vt == 0 else ())
                        first = False
                    T(lambda e, h=h, first=first: e.matmul(
                        pM[:, h * DK:(h + 1) * DK],
                        ab_s[:, h * 128:(h + 1) * 128],
                        kht_s[:, h * DK:(h + 1) * DK],
                        start=first, stop=True),
                      waits=([("V", M[("Aevac%d" % h, i)])] if first
                             else []) + [("V", M[("khat", i)])],
                      mark=("Ymm%d" % h, i))
            def pc_bm_postA(i):
                V(lambda e: e.tensor_tensor(yb_s[:], lam_s[:], pM[:],
                                            ALU.mult),
                  waits=[("T", M[("Ymm1", i)])], mark=("Ymerge", i),
                  drain=False)

            def pc_bm_post(i):
                b, tc = i // NT, i % NT
                xbuf = xb_s[:, (i % 6) * D:(i % 6 + 1) * D]
                x1buf = x1_s[:, sl(i)]
                pQb = pQ[:].bitcast(BF16)
                pMb = pM[:].bitcast(BF16)
                xrb = xr_s[:, sl2(i)]
                rsem = "sR" if i % 2 == 0 else "sR1"
                if dbg_h is not None:
                    dma("Y", "sD", lambda e, i=i: e.dma_start(
                        dbg2_h[i], yb_s[:]), waits=[("V", M[("Ymerge", i)])])
                    dma("Y", "sD", lambda e, i=i: e.dma_start(
                        dbg3_h[i], kht_s[:]), waits=[("V", M[("khat", i)])])

                pMb = pM[:].bitcast(BF16)
                for j in range(8):
                    T(lambda e, j=j, pMb=pMb: e.transpose(
                        pMb[:, j * 128:(j + 1) * 128],
                        yb_s[:, j * 128:(j + 1) * 128], idn_s[:]),
                      waits=[("V", M[("Ymerge", i)])] if j == 0 else (),
                      mark=("YT_t", i) if j == 7 else None)
                S(lambda e, pMb=pMb: e.activation(yt_s[:], pMb[:, 0:D],
                                                  AF.Copy),
                  waits=[("T", M[("YT_t", i)])], mark=("YTevac", i),
                  drain=False)
                for kt in range(8):
                    for half in range(2):
                        T(lambda e, kt=kt, half=half: e.matmul(
                            pM[:, half * DK:(half + 1) * DK],
                            yt_s[:, kt * 128:(kt + 1) * 128],
                            wo_s[:, kt * D + half * DK:
                                 kt * D + (half + 1) * DK],
                            start=(kt == 0), stop=(kt == 7)),
                          waits=[("S", M[("YTevac", i)])]
                          if (kt == 0 and half == 0) else (),
                          mark=("Womm", i)
                          if (kt == 7 and half == 1) else None)
                xrb = xr_s[:, sl2(i)]
                rsem = "sR" if i % 2 == 0 else "sR1"
                V(lambda e, xbuf=xbuf, xrb=xrb: e.tensor_add(
                    xrb, xbuf, pM[:]),
                  waits=[("T", M[("Womm", i)])] +
                        ([(rsem, 16 * (i // 2))] if i >= 2 else []) +
                        ([("sD", 48 * i)] if (dbg_h is not None and i >= 1)
                         else []),
                  mark=("xfree", i))
                dma("Y", rsem, lambda e, b=b, tc=tc, xrb=xrb: e.dma_start(
                    xr_h[b, tc], xrb),
                    waits=[("V", M[("xfree", i)])])
                if dbg_h is not None:
                    dma("Y", "sD", lambda e, b=b, tc=tc, xrb=xrb: e.dma_start(
                        dbg_h[b, tc], xrb))
                S(lambda e, xrb=xrb: e.activation(
                    dmy_s[:], xrb, AF.Copy,
                    accum_out=st_s[:, C_SUM2:C_SUM2 + 1]),
                  waits=[("V", M[("xfree", i)])])
                S(lambda e, xrb=xrb: e.activation(
                    dmy_s[:], xrb, AF.Square,
                    accum_out=st_s[:, C_SSQ2:C_SSQ2 + 1]),
                  mark=("ln2_stats", i), drain=False)
                V(lambda e, xrb=xrb: e.reduce_max(st_s[:, C_RMAX:C_RMAX + 1],
                                                  xrb, AXX), drain=False)
                V(lambda e, xrb=xrb: e.tensor_reduce(
                    st_s[:, C_RMIN:C_RMIN + 1], xrb, AXX, ALU.min),
                  drain=False)
                V(lambda e, i=i: e.tensor_scalar(
                    st_s[:, C_M2 + i:C_M2 + i + 1],
                    st_s[:, C_SUM2:C_SUM2 + 1], 1.0 / D, None, ALU.mult),
                  waits=[("S", M[("ln2_stats", i)])])
                V(lambda e: e.tensor_scalar(
                    st_s[:, C_U0:C_U0 + 1], st_s[:, C_SSQ2:C_SSQ2 + 1],
                    1.0 / D, None, ALU.mult), drain=False)
                V(lambda e, i=i: e.tensor_tensor(
                    st_s[:, C_U1:C_U1 + 1],
                    st_s[:, C_M2 + i:C_M2 + i + 1],
                    st_s[:, C_M2 + i:C_M2 + i + 1], ALU.mult))
                V(lambda e: e.tensor_sub(
                    st_s[:, C_U0:C_U0 + 1], st_s[:, C_U0:C_U0 + 1],
                    st_s[:, C_U1:C_U1 + 1]), mark=("ln2_var", i),
                  drain=False)
                S(lambda e, i=i: e.activation(
                    st_s[:, C_RS2 + i:C_RS2 + i + 1],
                    st_s[:, C_U0:C_U0 + 1], AF.Ln,
                    bias=st_s[:, C_E5:C_E5 + 1]),
                  waits=[("V", M[("ln2_var", i)])])
                S(lambda e, i=i: e.activation(
                    st_s[:, C_RS2 + i:C_RS2 + i + 1],
                    st_s[:, C_RS2 + i:C_RS2 + i + 1],
                    AF.Exp, scale=-0.5), mark=("ln2_rs", i), drain=False)
                V(lambda e, i=i: e.tensor_scalar(
                    st_s[:, C_U0:C_U0 + 1], st_s[:, C_RMAX:C_RMAX + 1],
                    st_s[:, C_M2 + i:C_M2 + i + 1], None, ALU.subtract),
                  waits=[("S", M[("ln2_rs", i)])], drain=False)
                V(lambda e, i=i: e.tensor_scalar(
                    st_s[:, C_U1:C_U1 + 1], st_s[:, C_RMIN:C_RMIN + 1],
                    -1.0, st_s[:, C_M2 + i:C_M2 + i + 1],
                    ALU.mult, ALU.add))
                V(lambda e: e.tensor_tensor(
                    st_s[:, C_U0:C_U0 + 1], st_s[:, C_U0:C_U0 + 1],
                    st_s[:, C_U1:C_U1 + 1], ALU.max))
                V(lambda e, i=i: e.tensor_scalar(
                    st_s[:, C_U0:C_U0 + 1], st_s[:, C_U0:C_U0 + 1],
                    st_s[:, C_RS2 + i:C_RS2 + i + 1], None, ALU.mult))
                V(lambda e: e.tensor_max(
                    st_s[:, C_VMAX:C_VMAX + 1],
                    st_s[:, C_VMAX:C_VMAX + 1],
                    st_s[:, C_U0:C_U0 + 1]), mark=("vmax", i),
                  drain=False)
            def pc_bm_Z(i):
                b, tc = i // NT, i % NT
                xbuf = xb_s[:, (i % 6) * D:(i % 6 + 1) * D]
                x1buf = x1_s[:, sl(i)]
                pQb = pQ[:].bitcast(BF16)
                pMb = pM[:].bitcast(BF16)
                xrb = xr_s[:, sl2(i)]
                rsem = "sR" if i % 2 == 0 else "sR1"
                for h in range(2):
                    Sh = (S0_s, S1_s)[h]
                    for vt in range(4):
                        pZ = (pK, pV)[vt // 2]
                        zsl = slice((vt % 2) * DK, (vt % 2 + 1) * DK)
                        T(lambda e, h=h, vt=vt, pZ=pZ, zsl=zsl: e.matmul(
                            pZ[:, zsl],
                            Vb_s[:, (i % 4) * D + h * DK + vt * 128:
                                 (i % 4) * D + h * DK + (vt + 1) * 128],
                            kht_s[:, h * DK:(h + 1) * DK],
                            start=True, stop=(tc == 0)),
                          waits=(([("S", M[("Kevac", min(i + 2, NI - 1))]),
                                   ("S", M[("Vevac", min(i + 2, NI - 1))])] +
                                  ([("V", M[("SmulH", 1, 1, i - 1)])]
                                   if i >= 1 else []))
                                 if (h == 0 and vt == 0) else []) +
                                ([("V", M[("SmulH", 0, 1, i)])]
                                 if (h == 1 and vt == 0) else []),
                          mark=("Zmm%d" % h, i)
                          if (vt == 3 and tc == 0) else None)
                        if tc > 0:
                            T(lambda e, vt=vt, pZ=pZ, zsl=zsl, Sh=Sh:
                              e.matmul(
                                  pZ[:, zsl], idn_s[:],
                                  Sh[:, vt * DK:(vt + 1) * DK],
                                  start=False, stop=True),
                              mark=("Zmm%d" % h, i) if vt == 3 else None)
                    for half in range(2):
                        pZ = (pK, pV)[half]
                        V(lambda e, h=h, half=half, pZ=pZ, Sh=Sh:
                          e.tensor_tensor(
                              Sh[:, half * 2 * DK:(half + 1) * 2 * DK]
                              .rearrange("p (r c) -> p r c", r=2),
                              pZ[:].rearrange("p (r c) -> p r c", r=2),
                              lcr_s[:, h * DK:(h + 1) * DK].unsqueeze(1)
                              .broadcast_to([128, 2, DK]), ALU.mult),
                          waits=[("T", M[("Zmm%d" % h, i)])]
                          if half == 0 else (),
                          mark=("SmulH", h, 1, i) if half == 1 else None,
                          drain=False)

            NPAIR = NI // 2
            stageA0(0)
            stageA0(1)

            wfc = {}
            tfc = {}
            W0 = NI - 8

            def wabs_chunk(p2):
                sl6 = (W0 - 2 + p2) % 6
                stg = xb_s[:, sl6 * D:(sl6 + 1) * D]
                fsem = "sF" if p2 % 2 == 0 else "sF1"
                dma("Y", fsem, lambda e, p2=p2, stg=stg: e.dma_start(
                    stg, wfT_h[p2 * 128:(p2 + 1) * 128, :]),
                    waits=[("V", M[("xfree", W0 - 2 + p2)])] +
                          ([("V", M[("wabs", p2 - 6)])] if p2 >= 6 else []))
                wfc[p2] = (fsem, dcnt[fsem])
                V(lambda e, stg=stg: e.reduce_sum(
                    st_s[:, 19:20], stg, AXX,
                    apply_absolute_value=True),
                  waits=[wfc[p2]])
                V(lambda e: e.tensor_add(
                    st_s[:, C_WABS:C_WABS + 1], st_s[:, C_WABS:C_WABS + 1],
                    st_s[:, 19:20]), mark=("wabs", p2))

            TSL = (2, 3, 0, 1)

            def tern_load(j):
                sl6 = TSL[j % 4]
                stg = xb_s[:, sl6 * D:(sl6 + 1) * D]
                fsem = "sX%d" % sl6 if sl6 else "sX0"
                if j < 2:
                    w = [("V", M[("wabs", 4 + j)])]
                elif j < 4:
                    w = [("V", M[("xfree", 28 + j)])]
                else:
                    w = [("V", M[("tern", j - 4)])]
                dma("Y", fsem, lambda e, j=j, stg=stg: e.dma_start(
                    stg, wfT_h[j * 128:(j + 1) * 128, :]), waits=w)
                tfc[j] = (fsem, dcnt[fsem])

            def halfstep(c, a, bp):
                if c is not None:
                    pc_x1tr(c)
                if a is not None:
                    pc_bm_pre(a)
                    pc_bm_lc(a)
                if c is not None:
                    pc_kv(c)
                if c is not None and c + 2 < NI:
                    stageA0(c + 2)
                if bp is not None:
                    pc_norm(bp)
                if a is not None:
                    pc_bm_A(a)
                if bp is not None:
                    stageBp(bp)
                    stageBpost(bp)
                if c is not None:
                    pc_qt(c)
                if a is not None:
                    pc_bm_Y(a)
                    pc_bm_postA(a)
                    pc_bm_Z(a)
                if c is not None:
                    pc_vt(c)
                if a is not None:
                    pc_bm_post(a)
                if c is not None and c >= W0:
                    wabs_chunk(c - W0)
                if c == NI - 2:
                    tern_load(0)
                if c == NI - 1:
                    tern_load(1)
                if c is None and a is not None and a >= NI - 2:
                    tern_load(a - (NI - 2) + 2)

            for p in range(NPAIR + 1):
                c0 = 2 * p if p < NPAIR else None
                c1 = 2 * p + 1 if p < NPAIR else None
                halfstep(c0, 2 * (p - 1) if p >= 1 else None,
                         2 * p - 1 if p >= 1 else None)
                halfstep(c1, 2 * (p - 1) + 1 if p >= 1 else None,
                         2 * p if p < NPAIR else None)

            T(lambda e: e.transpose(
                pM[0:1, 0:128], st_s[:, C_VMAX:C_VMAX + 1], idf_s[:]),
              waits=[("V", M[("vmax", NI - 1)]), ("V", M[("wabs", 7)]),
                     ("V", M[("SmulH", 1, 1, NI - 1)])])
            T(lambda e: e.transpose(
                pM[0:1, 128:256], st_s[:, C_WABS:C_WABS + 1], idf_s[:]),
              mark=("bt", 0))
            V(lambda e: e.reduce_max(st_s[0:1, C_T0:C_T0 + 1],
                                     pM[0:1, 0:128], AXX),
              waits=[("T", M[("bt", 0)])])
            V(lambda e: e.reduce_sum(st_s[0:1, C_T1:C_T1 + 1],
                                     pM[0:1, 128:256], AXX))
            V(lambda e: e.memset(t8_s[:], 1.0))
            V(lambda e: e.tensor_scalar(
                t8_s[0:1, 0:4], t8_s[0:1, 0:4], st_s[0:1, C_T0:C_T0 + 1],
                None, ALU.mult))
            V(lambda e: e.tensor_scalar(
                t8_s[0:1, 4:8], t8_s[0:1, 4:8], st_s[0:1, C_T1:C_T1 + 1],
                None, ALU.mult), mark=("t8", 0))
            dma("Y", "sC", lambda e: e.dma_start(cci_h, t8_s[:]),
                waits=[("V", M[("t8", 0)])])
            G(lambda e: e.collective_compute(
                "AllReduce", ALU.max, replica_groups=[list(range(cc_ranks))],
                ins=[cci_h], outs=[cco_h]),
              waits=[("sC", dcnt["sC"])], mark=("cc", 0))
            dma("G", "sCg", lambda e: e.dma_start(
                gbc_s[:], cco_h.partition_broadcast(128)[:, 0, :]),
                waits=[("G", M[("cc", 0)])])
            wait_only("V", [("sCg", dcnt["sCg"])])
            V(lambda e: e.tensor_copy(st_s[:, C_SC:C_SC + 1], gbc_s[:, 0:1]))
            V(lambda e: e.reciprocal(st_s[:, C_SC + 1:C_SC + 2],
                                     gbc_s[:, 0:1]))
            V(lambda e: e.tensor_scalar(
                st_s[:, C_SC + 1:C_SC + 2], st_s[:, C_SC + 1:C_SC + 2],
                127.0, None, ALU.mult))
            V(lambda e: e.tensor_scalar(
                st_s[:, C_SC + 2:C_SC + 3], gbc_s[:, 4:5],
                1.0 / (1024.0 * 1024.0), None, ALU.mult))
            V(lambda e: e.tensor_scalar(
                st_s[:, C_SC + 3:C_SC + 4], st_s[:, C_SC + 2:C_SC + 3],
                0.5, None, ALU.mult))
            V(lambda e: e.tensor_tensor(
                st_s[:, C_SC + 2:C_SC + 3], st_s[:, C_SC + 2:C_SC + 3],
                st_s[:, C_SC:C_SC + 1], ALU.mult))
            V(lambda e: e.tensor_scalar(
                st_s[:, C_SC + 2:C_SC + 3], st_s[:, C_SC + 2:C_SC + 3],
                1.0 / 127.0, None, ALU.mult), mark=("scbc", 0))
            terP = (Kb_s[:].bitcast(F32)[:, 0:D],
                    S0_s[:].bitcast(F32)[:, 0:D])
            sgnP = (QT_s[:].bitcast(F32)[:, 0:D],
                    QT_s[:].bitcast(F32)[:, D:2 * D])
            for p2 in range(8):
                if p2 >= 4:
                    tern_load(p2)
                ab, sg = terP[p2 % 2], sgnP[p2 % 2]
                sl6 = TSL[p2 % 4]
                stg = xb_s[:, sl6 * D:(sl6 + 1) * D]
                S(lambda e, stg=stg, ab=ab: e.activation(ab, stg, AF.Abs),
                  waits=[tfc[p2]] +
                        ([("V", M[("tern", p2 - 2)])] if p2 >= 2 else
                         [("V", M[("scbc", 0)])]),
                  drain=False)
                S(lambda e, stg=stg, sg=sg: e.activation(sg, stg, AF.Sign),
                  mark=("tern_s", p2), drain=False)
                V(lambda e, ab=ab: e.tensor_scalar(
                    ab, ab, st_s[:, C_SC + 3:C_SC + 4], None,
                    ALU.is_gt),
                  waits=[("S", M[("tern_s", p2)])])
                V(lambda e, p2=p2, ab=ab, sg=sg: e.tensor_tensor(
                    wf_s[:, p2 * D:(p2 + 1) * D], sg, ab,
                    ALU.mult), mark=("tern", p2), drain=False)

            C_RSIS = 96
            V(lambda e: e.tensor_scalar(
                st_s[:, C_RSIS:C_RSIS + 32], st_s[:, C_RS2:C_RS2 + 32],
                st_s[:, C_SC + 1:C_SC + 2], None, ALU.mult))
            V(lambda e: e.scalar_tensor_tensor(
                st_s[:, C_M2:C_M2 + 32], st_s[:, C_M2:C_M2 + 32], -1.0,
                st_s[:, C_RSIS:C_RSIS + 32], ALU.mult, ALU.mult))
            V(lambda e: e.reciprocal(st_s[:, C_T0:C_T0 + 1],
                                     st_s[:, C_SC + 2:C_SC + 3]))
            V(lambda e: e.tensor_scalar(
                vsc_s[:], bias_s[:], st_s[:, C_T0:C_T0 + 1], None, ALU.mult))
            V(lambda e: e.memset(dmy_s[0:1, 0:128], 1.0),
              waits=[("S", M[("tern_s", 7)])], mark=("p2prep", 0))

            KbF = Kb_s[:].bitcast(F32)
            S0F = S0_s[:].bitcast(F32)
            QTF = QT_s[:].bitcast(F32)
            x2nP = (KbF[:, 0:D], S0F[:, 0:D])
            i1P = (aarg_s[:].bitcast(I32), larg_s[:].bitcast(I32))
            uP = (aarg_s[:], larg_s[:])
            cP = (wb_s[:, 0:D], wb_s[:, D:2 * D])
            qinP = (Vb_s[:, 0:D], Vb_s[:, 2 * D:3 * D])
            qTP = (Vb_s[:, D:2 * D], Vb_s[:, 3 * D:4 * D])
            o1P = (QTF[:, 0:D], QTF[:, D:2 * D])
            pTP = (pM, pQ)
            pFP = (pK, pV)
            lqs = (("Y", "sL"), ("S", "sL1"), ("G", "sCg"))
            oqs = (("G", "sWg"), ("Y", "sO"), ("S", "sD"))
            p2lc = {}
            p2sc = {}
            sd0 = dcnt["sD"]

            def p2load(i):
                if i >= NI:
                    return
                b, tc = divmod(i, NT)
                q, sem = lqs[i % 3]
                xbuf = xb_s[:, (i % 3) * D:(i % 3 + 1) * D]
                dma(q, sem, lambda e, b=b, tc=tc, xbuf=xbuf:
                    e.dma_start(xbuf, xr_h[b, tc]),
                    waits=([("V", M[("p2free", i - 3)])] if i >= 3 else
                           [("V", M[("tern", 7)]), ("sR", dcnt["sR"]),
                            ("sR1", dcnt["sR1"])]))
                p2lc[i] = (sem, dcnt[sem])

            def p2store(i):
                b, tc = divmod(i, NT)
                q, sem = oqs[i % 3]
                o1b = o1P[i % 2]
                dma(q, sem, lambda e, b=b, tc=tc, o1b=o1b: e.dma_start(
                    out_h[b, tc], o1b),
                    waits=[("V", M[("p2free", i)])] +
                          ([("sD", sd0)] if (i % 3 == 2 and i < 5)
                           else []))
                p2sc[i] = (sem, dcnt[sem])

            def p2chainA(i):
                if i >= NI:
                    return
                p = i % 2
                xbuf = xb_s[:, (i % 3) * D:(i % 3 + 1) * D]
                x2n, i1b = x2nP[p], i1P[p]
                V(lambda e, i=i, xbuf=xbuf, x2n=x2n: e.tensor_scalar(
                    x2n, xbuf, st_s[:, C_RSIS + i:C_RSIS + i + 1],
                    st_s[:, C_M2 + i:C_M2 + i + 1], ALU.mult, ALU.add),
                  waits=[p2lc[i]] + ([("T", M[("bt", 0)])] if i < 2 else []),
                  mark=("x2n", i), drain=False)
                S(lambda e, x2n=x2n, i1b=i1b: e.activation(i1b, x2n, AF.Copy),
                  waits=[("V", M[("x2n", i)])], mark=("i1", i), drain=False)

            def p2chainB(i):
                if i >= NI:
                    return
                p = i % 2
                x2n, i1b, cb, qin = x2nP[p], i1P[p], cP[p], qinP[p]
                V(lambda e, x2n=x2n, i1b=i1b, cb=cb: e.scalar_tensor_tensor(
                    cb, i1b, 0.5, x2n, ALU.add, ALU.is_le),
                  waits=[("S", M[("i1", i)])], drain=False)
                V(lambda e, x2n=x2n, i1b=i1b: e.scalar_tensor_tensor(
                    x2n, i1b, -0.5, x2n, ALU.add, ALU.is_ge))
                V(lambda e, i1b=i1b, cb=cb, ub=uP[p]: e.tensor_tensor(
                    ub, i1b, cb, ALU.add))
                V(lambda e, qin=qin, x2n=x2n, ub=uP[p]: e.scalar_tensor_tensor(
                    qin, x2n, -1.0, ub, ALU.mult, ALU.add),
                  waits=([("T", M[("qT_t", i - 2)])] if i >= 2 else
                         [("T", M[("bt", 0)])]),
                  mark=("qin", i), drain=False)

            def p2pe_a(i):
                p = i % 2
                pT, pF = pTP[p], pFP[p]
                pTb = pT[:].bitcast(BF16)
                qin, qT = qinP[p], qTP[p]
                for j in range(8):
                    T(lambda e, j=j, pTb=pTb, qin=qin: e.transpose(
                        pTb[:, j * 128:(j + 1) * 128],
                        qin[:, j * 128:(j + 1) * 128], idn_s[:]),
                      waits=([("V", M[("qin", i)])] +
                             ([("S", M[("qT", i - 2)])] if i >= 2 else
                              ([("S", M[("QTevac", NI - 1)])] if i == 1
                               else []))) if j == 0 else (),
                      mark=("qT_t", i) if j == 7 else None)
                S(lambda e, qT=qT, pTb=pTb: e.activation(
                    qT, pTb[:, 0:D], AF.Copy),
                  waits=[("T", M[("qT_t", i)])] +
                        ([("T", M[("ffmm", i - 2)])] if i >= 2 else []),
                  mark=("qT", i), drain=False)
                for half in range(2):
                    T(lambda e, half=half, pF=pF: e.matmul(
                        pF[:, half * DK:(half + 1) * DK],
                        dmy_s[0:1, 0:128],
                        vsc_s[0:1, half * DK:(half + 1) * DK],
                        start=True, stop=False),
                      waits=(([("V", M[("p2free", i - 2)])] if i >= 2 else
                              [("V", M[("p2prep", 0)])])
                             if half == 0 else ()))
                for kt in range(8):
                    for half in range(2):
                        T(lambda e, kt=kt, half=half, pF=pF, qT=qT: e.matmul(
                            pF[:, half * DK:(half + 1) * DK],
                            qT[:, kt * 128:(kt + 1) * 128],
                            wf_s[:, kt * D + half * DK:
                                 kt * D + (half + 1) * DK],
                            start=False, stop=(kt == 7)),
                          waits=[("S", M[("qT", i)])]
                          if (kt == 0 and half == 0) else (),
                          mark=("ffmm", i) if (kt == 7 and half == 1)
                          else None)

            def p2pe_b(i):
                pF = pFP[i % 2]
                o1b = o1P[i % 2]
                xbuf = xb_s[:, (i % 3) * D:(i % 3 + 1) * D]
                V(lambda e, o1b=o1b, pF=pF, xbuf=xbuf: e.scalar_tensor_tensor(
                    o1b, pF[:], st_s[:, C_SC + 2:C_SC + 3],
                    xbuf, ALU.mult, ALU.add),
                  waits=[("T", M[("ffmm", i)])] +
                        ([p2sc[i - 2]] if i >= 2 else []),
                  mark=("p2free", i), drain=False)

            p2load(0)
            p2load(1)
            p2load(2)
            p2chainA(0)
            p2chainB(0)
            p2chainA(1)
            p2chainB(1)
            for i in range(NI):
                p2pe_a(i)
                p2chainA(i + 2)
                p2pe_b(i)
                p2chainB(i + 2)
                p2load(i + 3)
                p2store(i)
            wait_only("Y", [("sO", dcnt["sO"]), ("sWg", dcnt["sWg"]),
                            ("sD", dcnt["sD"])])

        @blk.sync
        def _(eng):
            program("Y", eng)

        @blk.gpsimd
        def _(eng):
            program("G", eng)

        @blk.scalar
        def _(eng):
            program("S", eng)

        @blk.vector
        def _(eng):
            program("V", eng)

        @blk.tensor
        def _(eng):
            program("T", eng)

    es.close()
    return nc


_NC_CACHE = {}


def kernel(L0_attn_x, L0_attn_Wq, L0_attn_Wk, L0_attn_Wv, L0_attn_Wo,
           L0_attn_W_gamma, L0_attn_b_gamma, L0_attn_W_lambda,
           L0_attn_b_lambda, L0_ffn_weight, L0_ffn_bias):
    x = np.ascontiguousarray(np.asarray(L0_attn_x, dtype=np.float32))
    assert x.shape == (16, 2048, 1024)

    def t(w):
        return np.ascontiguousarray(np.asarray(w, dtype=np.float32).T)

    wqT, wkT = t(L0_attn_Wq), t(L0_attn_Wk)
    wvT, woT = t(L0_attn_Wv), t(L0_attn_Wo)
    wfT = t(L0_ffn_weight)

    def rep(v):
        return np.ascontiguousarray(np.broadcast_to(
            np.asarray(v, np.float32).reshape(1, D), (128, D)))

    wgr, bgr = rep(L0_attn_W_gamma), rep(L0_attn_b_gamma)
    wlr, blr = rep(L0_attn_W_lambda), rep(L0_attn_b_lambda)
    biasr = rep(L0_ffn_bias)
    jj, tt = np.meshgrid(np.arange(128), np.arange(128), indexing="ij")
    tri = (jj <= tt).astype(BF)
    idn = np.eye(128, dtype=BF)
    sel = np.ones((128, 128), dtype=BF)

    if "nc" not in _NC_CACHE:
        _NC_CACHE["nc"] = build_nc()
    nc = _NC_CACHE["nc"]

    in_maps = []
    for c in range(NCORE):
        xs = np.ascontiguousarray(
            x[c * NB:(c + 1) * NB].reshape(NB, NT, TT, D))
        in_maps.append({
            "x": xs, "wqT": wqT, "wkT": wkT, "wvT": wvT, "woT": woT,
            "wfT": wfT, "wgr": wgr, "wlr": wlr, "bgr": bgr, "blr": blr,
            "biasr": biasr, "tri": tri, "idn": idn, "sel127": sel,
            "idnf": np.eye(128, dtype=np.float32),
        })
    res = run_bass_kernel_spmd(nc, in_maps, core_ids=list(range(NCORE)))
    out = np.empty((16, 2048, 1024), np.float32)
    for c in range(NCORE):
        out[c * NB:(c + 1) * NB] = np.asarray(
            res.results[c]["out"], np.float32).reshape(NB, 2048, D)
    return out


# version 36
# speedup vs baseline: 1.1206x; 1.1206x over previous
import numpy as np
import ml_dtypes
import concourse.bass as bass
import concourse.mybir as mybir
from concourse.bass_utils import run_bass_kernel_spmd

F32 = mybir.dt.float32
BF16 = mybir.dt.bfloat16
I32 = mybir.dt.int32
AF = mybir.ActivationFunctionType
ALU = mybir.AluOpType
AXX = mybir.AxisListType.X
BF = ml_dtypes.bfloat16

NB = 2
NT = 16
TT = 128
D = 1024
DK = 512
NCORE = 8
NI = NB * NT


def build_nc(cc_ranks=NCORE, dbg=False):
    nc = bass.Bass("TRN2", target_bir_lowering=False)

    x_h = nc.dram_tensor("x", [NB, NT, TT, D], F32, kind="ExternalInput").ap()
    wqT_h = nc.dram_tensor("wqT", [D, D], F32, kind="ExternalInput").ap()
    wkT_h = nc.dram_tensor("wkT", [D, D], F32, kind="ExternalInput").ap()
    wvT_h = nc.dram_tensor("wvT", [D, D], F32, kind="ExternalInput").ap()
    woT_h = nc.dram_tensor("woT", [D, D], F32, kind="ExternalInput").ap()
    wfT_h = nc.dram_tensor("wfT", [D, D], F32, kind="ExternalInput").ap()
    wgr_h = nc.dram_tensor("wgr", [128, D], F32, kind="ExternalInput").ap()
    wlr_h = nc.dram_tensor("wlr", [128, D], F32, kind="ExternalInput").ap()
    bgr_h = nc.dram_tensor("bgr", [128, D], F32, kind="ExternalInput").ap()
    blr_h = nc.dram_tensor("blr", [128, D], F32, kind="ExternalInput").ap()
    bias_h = nc.dram_tensor("biasr", [128, D], F32, kind="ExternalInput").ap()
    tri_h = nc.dram_tensor("tri", [128, 128], BF16, kind="ExternalInput").ap()
    idn_h = nc.dram_tensor("idn", [128, 128], BF16, kind="ExternalInput").ap()
    sel_h = nc.dram_tensor("sel127", [128, 128], BF16, kind="ExternalInput").ap()
    idf_h = nc.dram_tensor("idnf", [128, 128], F32, kind="ExternalInput").ap()

    xr_h = nc.dram_tensor("xr_spill", [NB, NT, TT, D], F32, kind="Internal").ap()
    cci_h = nc.dram_tensor("cc_i", [1, 8], F32, kind="Internal").ap()
    cco_h = nc.dram_tensor("cc_o", [1, 8], F32, kind="Internal").ap()
    out_h = nc.dram_tensor("out", [NB, NT, TT, D], F32, kind="ExternalOutput").ap()
    dbg_h = (nc.dram_tensor("dbg", [NB, NT, TT, D], F32, kind="ExternalOutput").ap()
             if dbg else None)
    dbg2_h = (nc.dram_tensor("dbg2", [NI, TT, D], BF16, kind="ExternalOutput").ap()
              if dbg else None)
    dbg3_h = (nc.dram_tensor("dbg3", [NI, TT, D], BF16, kind="ExternalOutput").ap()
              if dbg else None)
    dbg4_h = (nc.dram_tensor("dbg4", [NI, TT, D], BF16, kind="ExternalOutput").ap()
              if dbg else None)

    import contextlib
    es = contextlib.ExitStack()
    wq_s = es.enter_context(nc.sbuf_tensor("wq_s", [128, 8 * D], BF16))
    wk_s = es.enter_context(nc.sbuf_tensor("wk_s", [128, 8 * D], BF16))
    wv_s = es.enter_context(nc.sbuf_tensor("wv_s", [128, 8 * D], BF16))
    wo_s = es.enter_context(nc.sbuf_tensor("wo_s", [128, 8 * D], BF16))
    wgr_s = es.enter_context(nc.sbuf_tensor("wgr_s", [128, D], BF16))
    wlr_s = es.enter_context(nc.sbuf_tensor("wlr_s", [128, D], BF16))
    bgr_s = es.enter_context(nc.sbuf_tensor("bgr_s", [128, D], F32))
    blr_s = es.enter_context(nc.sbuf_tensor("blr_s", [128, D], F32))
    bias_s = es.enter_context(nc.sbuf_tensor("bias_s", [128, D], F32))
    tri_s = es.enter_context(nc.sbuf_tensor("tri_s", [128, 128], BF16))
    idn_s = es.enter_context(nc.sbuf_tensor("idn_s", [128, 128], BF16))
    sel_s = es.enter_context(nc.sbuf_tensor("sel_s", [128, 128], BF16))
    idf_s = es.enter_context(nc.sbuf_tensor("idf_s", [128, 128], F32))
    gbc_s = es.enter_context(nc.sbuf_tensor("gbc_s", [128, 8], F32))
    S0_s = es.enter_context(nc.sbuf_tensor("S0_s", [128, 4 * DK], BF16))
    S1_s = es.enter_context(nc.sbuf_tensor("S1_s", [128, 4 * DK], BF16))
    xb_s = es.enter_context(nc.sbuf_tensor("xb_s", [128, 6 * D], F32))
    x1_s = es.enter_context(nc.sbuf_tensor("x1_s", [128, 4 * D], BF16))
    x1T_s = es.enter_context(nc.sbuf_tensor("x1T_s", [128, D], BF16))
    Kb_s = es.enter_context(nc.sbuf_tensor("Kb_s", [128, 2 * D], BF16))
    Vb_s = es.enter_context(nc.sbuf_tensor("Vb_s", [128, 4 * D], BF16))
    QT_s = es.enter_context(nc.sbuf_tensor("QT_s", [128, 4 * D], BF16))
    VT_s = es.enter_context(nc.sbuf_tensor("VT_s", [128, 4 * D], BF16))
    kn_s = es.enter_context(nc.sbuf_tensor("kn_s", [128, D], BF16))
    ktl_s = es.enter_context(nc.sbuf_tensor("ktl_s", [128, 4 * D], BF16))
    kht_s = es.enter_context(nc.sbuf_tensor("kht_s", [128, D], BF16))
    logr_s = es.enter_context(nc.sbuf_tensor("logr_s", [128, 2 * D], BF16))
    lam_s = es.enter_context(nc.sbuf_tensor("lam_s", [128, D], BF16))
    e2_s = es.enter_context(nc.sbuf_tensor("e2_s", [128, D], BF16))
    aarg_s = es.enter_context(nc.sbuf_tensor("aarg_s", [128, D], F32))
    larg_s = es.enter_context(nc.sbuf_tensor("larg_s", [128, D], F32))
    wb_s = es.enter_context(nc.sbuf_tensor("wb_s", [128, 2 * D], F32))
    ab_s = es.enter_context(nc.sbuf_tensor("ab_s", [128, 256], BF16))
    lcr_s = es.enter_context(nc.sbuf_tensor("lcr_s", [128, D], BF16))
    yb_s = es.enter_context(nc.sbuf_tensor("yb_s", [128, D], BF16))
    yt_s = es.enter_context(nc.sbuf_tensor("yt_s", [128, D], BF16))
    xr_s = es.enter_context(nc.sbuf_tensor("xr_s", [128, 2 * D], F32))
    vsc_s = es.enter_context(nc.sbuf_tensor("vsc_s", [128, D], BF16))
    dmy_s = es.enter_context(nc.sbuf_tensor("dmy_s", [128, D], BF16))
    wf_s = wq_s
    st_s = es.enter_context(nc.sbuf_tensor("st_s", [128, 128], F32))
    x2n_s = Kb_s[:].bitcast(F32)[:, 0:D]
    o1_s = QT_s[:].bitcast(F32)[:, 0:D]
    qin_s = Vb_s[:, 0:D]
    qT_s = Vb_s[:, D:2 * D]
    t8_s = es.enter_context(nc.sbuf_tensor("t8_s", [1, 8], F32))
    g8_s = es.enter_context(nc.sbuf_tensor("g8_s", [1, 8], F32))
    pK = es.enter_context(nc.psum_tensor("pK", [128, D], F32))
    pV = es.enter_context(nc.psum_tensor("pV", [128, D], F32))
    pQ = es.enter_context(nc.psum_tensor("pQ", [128, D], F32))
    pM = es.enter_context(nc.psum_tensor("pM", [128, D], F32))
    sT = es.enter_context(nc.semaphore("sT"))
    sV = es.enter_context(nc.semaphore("sV"))
    sS = es.enter_context(nc.semaphore("sS"))
    sG = es.enter_context(nc.semaphore("sG"))
    sX = es.enter_context(nc.semaphore("sX"))
    sW = es.enter_context(nc.semaphore("sW"))
    sR = es.enter_context(nc.semaphore("sR"))
    sL = es.enter_context(nc.semaphore("sL"))
    sO = es.enter_context(nc.semaphore("sO"))
    sC = es.enter_context(nc.semaphore("sC"))
    sF = es.enter_context(nc.semaphore("sF"))
    sWg = es.enter_context(nc.semaphore("sWg"))
    sX1 = es.enter_context(nc.semaphore("sX1"))
    sL1 = es.enter_context(nc.semaphore("sL1"))
    sF1 = es.enter_context(nc.semaphore("sF1"))
    sX2 = es.enter_context(nc.semaphore("sX2"))
    sX3 = es.enter_context(nc.semaphore("sX3"))
    sX4 = es.enter_context(nc.semaphore("sX4"))
    sX5 = es.enter_context(nc.semaphore("sX5"))
    sD = es.enter_context(nc.semaphore("sD"))
    sR1 = es.enter_context(nc.semaphore("sR1"))
    sCg = es.enter_context(nc.semaphore("sCg"))
    blk = es.enter_context(nc.Block())
    if True:
        C_SUM, C_SSQ, C_M, C_RS, C_T0, C_T1, C_RMAX, C_RMIN = range(8)
        C_VMAX = 8
        C_WABS = 9
        C_SC = 10
        C_SSQH = 16
        C_RNH = 18
        C_SUM2, C_SSQ2 = 14, 15
        C_E5, C_E24, C_E38 = 20, 21, 22
        C_M2 = 32
        C_RS2 = 64

        sems = {"T": sT, "V": sV, "S": sS, "G": sG}
        dsem = {"sX": sX, "sW": sW, "sR": sR, "sL": sL, "sO": sO,
                "sC": sC, "sF": sF, "sWg": sWg, "sCg": sCg,
                "sX1": sX1, "sL1": sL1, "sF1": sF1,
                "sX0": sX, "sX2": sX2, "sX3": sX3, "sD": sD, "sR1": sR1,
                "sX4": sX4, "sX5": sX5}
        cnt = {"T": 0, "V": 0, "S": 0, "G": 0}
        dcnt = {k: 0 for k in dsem}
        marks = {}

        def program(me, eng):
            for k in cnt:
                cnt[k] = 0
            for k in dcnt:
                dcnt[k] = 0
            marks.clear()

            def _w(waits):
                for sn, val in waits:
                    if val > 0:
                        eng.wait_ge(sems.get(sn) or dsem[sn], val)

            def op(tag, fn, waits=(), mark=None, drain=True):
                if me == tag:
                    _w(waits)
                    inst = fn(eng)
                    if mark is not None:
                        inst.then_inc(sems[tag], 1)
                    if drain and tag in ("V", "S", "G"):
                        eng.drain()
                if mark is not None:
                    cnt[tag] += 1
                    marks[mark] = cnt[tag]

            def dma(tag, sname, fn, waits=()):
                if me == tag:
                    _w(waits)
                    fn(eng).then_inc(dsem[sname], 16)
                dcnt[sname] += 16

            def wait_only(tag, waits):
                if me == tag:
                    _w(waits)

            T = lambda *a, **k: op("T", *a, **k)
            V = lambda *a, **k: op("V", *a, **k)
            S = lambda *a, **k: op("S", *a, **k)
            G = lambda *a, **k: op("G", *a, **k)
            M = marks

            pMb = pM[:].bitcast(BF16)
            i1ap = aarg_s[:].bitcast(I32)

            for dst, src in [(wq_s, wqT_h), (wk_s, wkT_h), (wv_s, wvT_h),
                             (wo_s, woT_h)]:
                for et in range(8):
                    dma("G", "sWg", lambda e, dst=dst, src=src, et=et:
                        e.dma_start(dst[:, et * D:(et + 1) * D],
                                    src[et * 128:(et + 1) * 128, :]))
            for dst, src in [(wgr_s, wgr_h), (wlr_s, wlr_h)]:
                dma("G", "sWg", lambda e, dst=dst, src=src: e.dma_start(dst[:], src))
            for dst, src in [(bgr_s, bgr_h), (blr_s, blr_h), (bias_s, bias_h),
                             (tri_s, tri_h), (idn_s, idn_h), (sel_s, sel_h),
                             (idf_s, idf_h)]:
                dma("Y", "sW", lambda e, dst=dst, src=src: e.dma_start(dst[:], src))
            W_ALL = dcnt["sW"]
            WG_ALL = dcnt["sWg"]
            V(lambda e: e.memset(st_s[:, C_VMAX:C_VMAX + 1], 0.0))
            V(lambda e: e.memset(st_s[:, C_E5:C_E5 + 1], 1e-5))
            V(lambda e: e.memset(st_s[:, C_E24:C_E24 + 1], 1e-24))
            V(lambda e: e.memset(st_s[:, C_E38:C_E38 + 1], 1e-38))
            V(lambda e: e.memset(st_s[:, C_WABS:C_WABS + 1], 0.0),
              mark=("init", 0))

            C_SSQH2 = C_SSQH
            C_RNH4 = 24
            C_U0, C_U1 = 96, 97

            def sl(i):
                return slice((i % 4) * D, (i % 4 + 1) * D)

            def sl2(i):
                return slice((i % 2) * D, (i % 2 + 1) * D)

            def stageA0(i):
                b, tc = i // NT, i % NT
                xbuf = xb_s[:, (i % 6) * D:(i % 6 + 1) * D]
                xsem = "sX%d" % (i % 6)
                x1buf = x1_s[:, sl(i)]
                dma("Y", xsem,
                    lambda e, b=b, tc=tc, xbuf=xbuf: e.dma_start(
                        xbuf, x_h[b, tc]),
                    waits=([("V", M[("xfree", i - 6)])] if i >= 6 else []) +
                          ([("V", M[("wabs", i - 28)])] if i >= 30 else []))
                V(lambda e, xbuf=xbuf: e.reduce_sum(
                    st_s[:, C_SUM:C_SUM + 1], xbuf, AXX),
                  waits=[(xsem, 16 * (i // 6 + 1))] +
                        ([("V", M[("x1", i - 1)])] if i >= 1 else []))
                S(lambda e, xbuf=xbuf: e.activation(
                    dmy_s[:], xbuf, AF.Square,
                    accum_out=st_s[:, C_SSQ:C_SSQ + 1]),
                  waits=[(xsem, 16 * (i // 6 + 1)), ("sW", W_ALL),
                         ("sWg", WG_ALL)] +
                        ([("V", M[("x1", i - 1)])] if i >= 1 else []),
                  mark=("ln_stats", i), drain=False)
                V(lambda e: e.tensor_scalar(
                    st_s[:, C_M:C_M + 1], st_s[:, C_SUM:C_SUM + 1],
                    1.0 / D, None, ALU.mult))
                V(lambda e: e.tensor_scalar(
                    st_s[:, C_T0:C_T0 + 1], st_s[:, C_SSQ:C_SSQ + 1],
                    1.0 / D, None, ALU.mult),
                  waits=[("S", M[("ln_stats", i)])], drain=False)
                V(lambda e: e.tensor_tensor(
                    st_s[:, C_T1:C_T1 + 1], st_s[:, C_M:C_M + 1],
                    st_s[:, C_M:C_M + 1], ALU.mult))
                V(lambda e: e.tensor_sub(
                    st_s[:, C_T0:C_T0 + 1], st_s[:, C_T0:C_T0 + 1],
                    st_s[:, C_T1:C_T1 + 1]), mark=("ln_var", i),
                  drain=False)
                S(lambda e: e.activation(
                    st_s[:, C_RS:C_RS + 1], st_s[:, C_T0:C_T0 + 1],
                    AF.Ln, bias=st_s[:, C_E5:C_E5 + 1]),
                  waits=[("V", M[("ln_var", i)])])
                S(lambda e: e.activation(
                    st_s[:, C_RS:C_RS + 1], st_s[:, C_RS:C_RS + 1],
                    AF.Exp, scale=-0.5), mark=("ln_rs", i), drain=False)
                V(lambda e, xbuf=xbuf, x1buf=x1buf: e.tensor_scalar(
                    x1buf, xbuf, st_s[:, C_M:C_M + 1],
                    st_s[:, C_RS:C_RS + 1], ALU.subtract, ALU.mult),
                  waits=[("S", M[("ln_rs", i)])] +
                        ([("T", M[("x1T_t", i - 4)])] if i >= 4 else []),
                  mark=("x1", i), drain=False)

            def pc_x1tr(i):
                b, tc = i // NT, i % NT
                xbuf = xb_s[:, (i % 6) * D:(i % 6 + 1) * D]
                x1buf = x1_s[:, sl(i)]
                pQb = pQ[:].bitcast(BF16)
                pMb = pM[:].bitcast(BF16)
                xrb = xr_s[:, sl2(i)]
                rsem = "sR" if i % 2 == 0 else "sR1"
                pKb = pK[:].bitcast(BF16)
                for et in range(8):
                    T(lambda e, et=et, i=i, pKb=pKb: e.transpose(
                        pKb[:, et * 128:(et + 1) * 128],
                        x1_s[:, (i % 4) * D + et * 128:
                             (i % 4) * D + (et + 1) * 128], idn_s[:]),
                      waits=([("V", M[("x1", i)])] +
                             ([("V", M[("SmulH", 1, 1, i - 3)])]
                              if i >= 3 else []))
                      if et == 0 else (),
                      mark=("x1T_t", i) if et == 7 else None)
                S(lambda e, pKb=pKb: e.activation(x1T_s[:], pKb[:, 0:D],
                                                  AF.Copy),
                  waits=[("T", M[("x1T_t", i)])], mark=("x1T", i),
                  drain=False)
            def pc_kv(i):
                b, tc = i // NT, i % NT
                xbuf = xb_s[:, (i % 6) * D:(i % 6 + 1) * D]
                x1buf = x1_s[:, sl(i)]
                pQb = pQ[:].bitcast(BF16)
                pMb = pM[:].bitcast(BF16)
                xrb = xr_s[:, sl2(i)]
                rsem = "sR" if i % 2 == 0 else "sR1"
                for et in range(8):
                    for half in range(2):
                        T(lambda e, et=et, half=half: e.matmul(
                            pK[:, half * DK:(half + 1) * DK],
                            x1T_s[:, et * 128:(et + 1) * 128],
                            wk_s[:, et * D + half * DK:
                                 et * D + (half + 1) * DK],
                            start=(et == 0), stop=(et == 7)),
                          waits=([("S", M[("x1T", i)])] +
                                 ([("S", M[("Kevac", i - 1)]),
                                   ("S", M[("Vevac", i - 1)])]
                                  if i >= 1 else []) +
                                 ([("V", M[("SmulH", 1, 1, i - 3)])]
                                  if i - 3 >= 0 else []))
                          if (et == 0 and half == 0) else ())
                        T(lambda e, et=et, half=half: e.matmul(
                            pV[:, half * DK:(half + 1) * DK],
                            x1T_s[:, et * 128:(et + 1) * 128],
                            wv_s[:, et * D + half * DK:
                                 et * D + (half + 1) * DK],
                            start=(et == 0), stop=(et == 7)),
                          mark=("KVmm", i)
                          if (et == 7 and half == 1) else None)
                S(lambda e: e.activation(Kb_s[:, sl2(i)], pK[:], AF.Copy),
                  waits=[("T", M[("KVmm", i)])], mark=("Kevac", i),
                  drain=False)
                S(lambda e: e.activation(Vb_s[:, sl(i)], pV[:], AF.Copy),
                  waits=[("T", M[("KVmm", i)])], mark=("Vevac", i),
                  drain=False)

            def pc_qt(i):
                b, tc = i // NT, i % NT
                xbuf = xb_s[:, (i % 6) * D:(i % 6 + 1) * D]
                x1buf = x1_s[:, sl(i)]
                pQb = pQ[:].bitcast(BF16)
                pMb = pM[:].bitcast(BF16)
                xrb = xr_s[:, sl2(i)]
                rsem = "sR" if i % 2 == 0 else "sR1"
                for dt in range(8):
                    for et in range(8):
                        T(lambda e, dt=dt, et=et: e.matmul(
                            pQ[:, dt * 128:(dt + 1) * 128],
                            wq_s[:, et * D + dt * 128:
                                 et * D + (dt + 1) * 128],
                            x1T_s[:, et * 128:(et + 1) * 128],
                            start=(et == 0), stop=(et == 7)),
                          waits=([("S", M[("x1T", i)])] +
                                 ([("S", M[("lcr", i - 2)])] if i >= 2
                                  else []))
                          if (dt == 0 and et == 0) else (),
                          mark=("QTmm", i)
                          if (dt == 7 and et == 7) else None)
                S(lambda e: e.activation(QT_s[:, sl(i)], pQ[:], AF.Copy),
                  waits=[("T", M[("QTmm", i)])], mark=("QTevac", i),
                  drain=False)

            def pc_vt(i):
                b, tc = i // NT, i % NT
                xbuf = xb_s[:, (i % 6) * D:(i % 6 + 1) * D]
                x1buf = x1_s[:, sl(i)]
                pQb = pQ[:].bitcast(BF16)
                pMb = pM[:].bitcast(BF16)
                xrb = xr_s[:, sl2(i)]
                rsem = "sR" if i % 2 == 0 else "sR1"
                for j in range(8):
                    T(lambda e, j=j: e.transpose(
                        pQb[:, j * 128:(j + 1) * 128],
                        Vb_s[:, (i % 4) * D + j * 128:
                             (i % 4) * D + (j + 1) * 128], idn_s[:]),
                      waits=[("S", M[("Vevac", i)]),
                             ("S", M[("QTevac", i)])] if j == 0 else (),
                      mark=("VT_t", i) if j == 7 else None)
                S(lambda e: e.activation(VT_s[:, sl(i)], pQb[:, 0:D],
                                         AF.Copy),
                  waits=[("T", M[("VT_t", i)])], mark=("VTevac", i),
                  drain=False)
            def pc_norm(i):
                b, tc = i // NT, i % NT
                xbuf = xb_s[:, (i % 6) * D:(i % 6 + 1) * D]
                x1buf = x1_s[:, sl(i)]
                pQb = pQ[:].bitcast(BF16)
                pMb = pM[:].bitcast(BF16)
                xrb = xr_s[:, sl2(i)]
                rsem = "sR" if i % 2 == 0 else "sR1"
                for h in range(2):
                    S(lambda e, h=h: e.activation(
                        dmy_s[:, h * DK:(h + 1) * DK],
                        Kb_s[:, (i % 2) * D + h * DK:
                             (i % 2) * D + (h + 1) * DK],
                        AF.Square,
                        accum_out=st_s[:, C_SSQH2 + h:C_SSQH2 + h + 1]),
                      waits=[("S", M[("Kevac", i)])] if h == 0 else (),
                      drain=(h == 1))
                S(lambda e: e.activation(
                    st_s[:, C_RNH4 + (i % 4) * 2:C_RNH4 + (i % 4) * 2 + 2],
                    st_s[:, C_SSQH2:C_SSQH2 + 2],
                    AF.Ln, bias=st_s[:, C_E24:C_E24 + 1]))
                S(lambda e: e.activation(
                    st_s[:, C_RNH4 + (i % 4) * 2:C_RNH4 + (i % 4) * 2 + 2],
                    st_s[:, C_RNH4 + (i % 4) * 2:C_RNH4 + (i % 4) * 2 + 2],
                    AF.Exp, scale=-0.5), mark=("rnorm", i), drain=False)

            def stageBp(i):
                for h in range(2):
                    V(lambda e, h=h: e.tensor_scalar(
                        kn_s[:, h * DK:(h + 1) * DK],
                        Kb_s[:, (i % 2) * D + h * DK:
                             (i % 2) * D + (h + 1) * DK],
                        st_s[:, C_RNH4 + (i % 4) * 2 + h:
                             C_RNH4 + (i % 4) * 2 + h + 1], None, ALU.mult),
                      waits=([("S", M[("rnorm", i)])] +
                             ([("V", M[("ktil", i - 1)])] if i >= 1 else []))
                      if h == 0 else (),
                      drain=(h == 1))
                V(lambda e: e.tensor_mul(aarg_s[:], kn_s[:], wgr_s[:]),
                  drain=False)
                V(lambda e: e.tensor_mul(larg_s[:], kn_s[:], wlr_s[:]))
                V(lambda e: e.tensor_add(aarg_s[:], aarg_s[:], bgr_s[:]),
                  drain=False)
                V(lambda e: e.tensor_add(larg_s[:], larg_s[:], blr_s[:]),
                  mark=("args", i), drain=False)
                S(lambda e: e.activation(aarg_s[:], aarg_s[:], AF.Sigmoid),
                  waits=[("V", M[("args", i)])], drain=False)
                S(lambda e: e.activation(larg_s[:], larg_s[:], AF.Sigmoid),
                  mark=("sig", i), drain=False)
                V(lambda e, i=i: e.scalar_tensor_tensor(
                    wb_s[:, sl2(i)], larg_s[:], 1.0, aarg_s[:],
                    ALU.subtract, ALU.mult),
                  waits=[("S", M[("sig", i)])] +
                        ([("S", M[("logr", i - 2)])] if i >= 2 else []),
                  mark=("w", i))
                V(lambda e, i=i: e.scalar_tensor_tensor(
                    ktl_s[:, sl(i)], wb_s[:, sl2(i)], -1.0, kn_s[:],
                    ALU.subtract, ALU.mult),
                  waits=([("V", M[("khat", i - 4)])] if i >= 4 else []),
                  mark=("ktil", i), drain=False)

            def stageBpost(i):
                S(lambda e, i=i: e.activation(
                    logr_s[:, sl2(i)], wb_s[:, sl2(i)], AF.Ln,
                    scale=-1.0, bias=st_s[:, C_E38:C_E38 + 1]),
                  waits=[("V", M[("w", i)])] +
                        ([("T", M[("Lmm", i - 2)])] if i >= 2 else []),
                  mark=("logr", i), drain=False)

            def pc_bm_pre(i):
                b, tc = i // NT, i % NT
                xbuf = xb_s[:, (i % 6) * D:(i % 6 + 1) * D]
                x1buf = x1_s[:, sl(i)]
                pQb = pQ[:].bitcast(BF16)
                pMb = pM[:].bitcast(BF16)
                xrb = xr_s[:, sl2(i)]
                rsem = "sR" if i % 2 == 0 else "sR1"
                for half in range(2):
                    T(lambda e, half=half, i=i: e.matmul(
                        pM[:, half * DK:(half + 1) * DK], tri_s[:],
                        logr_s[:, (i % 2) * D + half * DK:
                               (i % 2) * D + (half + 1) * DK],
                        start=True, stop=True),
                      waits=([("S", M[("logr", i)])] +
                             ([("V", M[("xfree", i - 1)])]
                              if i >= 1 else []))
                      if half == 0 else (),
                      mark=("Lmm", i) if half == 1 else None)
                S(lambda e: e.activation(lam_s[:], pM[:], AF.Exp),
                  waits=[("T", M[("Lmm", i)])], drain=False)
                S(lambda e: e.activation(e2_s[:], pM[:], AF.Exp,
                                         scale=-1.0), mark=("Lexps", i),
                  drain=False)
                V(lambda e, i=i: e.tensor_mul(kht_s[:], ktl_s[:, sl(i)],
                                              e2_s[:]),
                  waits=[("S", M[("Lexps", i)]), ("V", M[("ktil", i)])] +
                        ([("sD", 48 * i)] if (dbg_h is not None and i >= 1)
                         else []),
                  mark=("khat", i), drain=False)
            def pc_bm_lc(i):
                b, tc = i // NT, i % NT
                xbuf = xb_s[:, (i % 6) * D:(i % 6 + 1) * D]
                x1buf = x1_s[:, sl(i)]
                pQb = pQ[:].bitcast(BF16)
                pMb = pM[:].bitcast(BF16)
                xrb = xr_s[:, sl2(i)]
                rsem = "sR" if i % 2 == 0 else "sR1"
                for h in range(2):
                    T(lambda e, h=h, i=i: e.matmul(
                        pQ[:, h * DK:(h + 1) * DK], sel_s[:],
                        logr_s[:, (i % 2) * D + h * DK:
                               (i % 2) * D + (h + 1) * DK],
                        start=True, stop=True),
                      waits=[("S", M[("logr", i)]),
                             ("S", M[("VTevac", min(i + 1, NI - 1))])]
                      if h == 0 else (),
                      mark=("lcmm", i) if h == 1 else None)
                S(lambda e: e.activation(lcr_s[:], pQ[:], AF.Exp),
                  waits=[("T", M[("lcmm", i)])] +
                        ([("V", M[("SmulH", 1, 1, i - 1)])]
                         if i >= 1 else []),
                  mark=("lcr", i), drain=False)
            def pc_bm_A(i):
                b, tc = i // NT, i % NT
                xbuf = xb_s[:, (i % 6) * D:(i % 6 + 1) * D]
                x1buf = x1_s[:, sl(i)]
                pQb = pQ[:].bitcast(BF16)
                pMb = pM[:].bitcast(BF16)
                xrb = xr_s[:, sl2(i)]
                rsem = "sR" if i % 2 == 0 else "sR1"
                for h in range(2):
                    for ct in range(4):
                        T(lambda e, h=h, ct=ct: e.matmul(
                            pM[:, h * DK: h * DK + 128],
                            VT_s[:, (i % 4) * D + (h * 4 + ct) * 128:
                                 (i % 4) * D + (h * 4 + ct + 1) * 128],
                            QT_s[:, (i % 4) * D + (h * 4 + ct) * 128:
                                 (i % 4) * D + (h * 4 + ct + 1) * 128],
                            start=(ct == 0), stop=(ct == 3)),
                          waits=[("S", M[("lcr", i)]),
                                 ("S", M[("Lexps", i)])]
                          if (h == 0 and ct == 0) else (),
                          mark=("Amm%d" % h, i) if ct == 3 else None)
                    V(lambda e, h=h: e.tensor_tensor(
                        ab_s[:, h * 128:(h + 1) * 128],
                        pM[:, h * DK: h * DK + 128], tri_s[:], ALU.mult),
                      waits=[("T", M[("Amm%d" % h, i)])],
                      mark=("Aevac%d" % h, i), drain=False)
            def pc_bm_Y(i):
                b, tc = i // NT, i % NT
                xbuf = xb_s[:, (i % 6) * D:(i % 6 + 1) * D]
                x1buf = x1_s[:, sl(i)]
                pQb = pQ[:].bitcast(BF16)
                pMb = pM[:].bitcast(BF16)
                xrb = xr_s[:, sl2(i)]
                rsem = "sR" if i % 2 == 0 else "sR1"
                for h in range(2):
                    Sh = (S0_s, S1_s)[h]
                    first = True
                    if tc > 0:
                        for vt in range(4):
                            T(lambda e, h=h, vt=vt, Sh=Sh: e.matmul(
                                pM[:, h * DK:(h + 1) * DK],
                                QT_s[:, (i % 4) * D + (h * 4 + vt) * 128:
                                     (i % 4) * D + (h * 4 + vt + 1) * 128],
                                Sh[:, vt * DK:(vt + 1) * DK],
                                start=(vt == 0), stop=False),
                              waits=[("V", M[("Aevac%d" % h, i)]),
                                     ("V", M[("SmulH", 1, 1, i - 1)])]
                              if vt == 0 else ())
                        first = False
                    T(lambda e, h=h, first=first: e.matmul(
                        pM[:, h * DK:(h + 1) * DK],
                        ab_s[:, h * 128:(h + 1) * 128],
                        kht_s[:, h * DK:(h + 1) * DK],
                        start=first, stop=True),
                      waits=([("V", M[("Aevac%d" % h, i)])] if first
                             else []) + [("V", M[("khat", i)])],
                      mark=("Ymm%d" % h, i))
            def pc_bm_postA(i):
                V(lambda e: e.tensor_tensor(yb_s[:], lam_s[:], pM[:],
                                            ALU.mult),
                  waits=[("T", M[("Ymm1", i)])], mark=("Ymerge", i),
                  drain=False)

            def pc_bm_post(i):
                b, tc = i // NT, i % NT
                xbuf = xb_s[:, (i % 6) * D:(i % 6 + 1) * D]
                x1buf = x1_s[:, sl(i)]
                pQb = pQ[:].bitcast(BF16)
                pMb = pM[:].bitcast(BF16)
                xrb = xr_s[:, sl2(i)]
                rsem = "sR" if i % 2 == 0 else "sR1"
                if dbg_h is not None:
                    dma("Y", "sD", lambda e, i=i: e.dma_start(
                        dbg2_h[i], yb_s[:]), waits=[("V", M[("Ymerge", i)])])
                    dma("Y", "sD", lambda e, i=i: e.dma_start(
                        dbg3_h[i], kht_s[:]), waits=[("V", M[("khat", i)])])

                pMb = pM[:].bitcast(BF16)
                for j in range(8):
                    T(lambda e, j=j, pMb=pMb: e.transpose(
                        pMb[:, j * 128:(j + 1) * 128],
                        yb_s[:, j * 128:(j + 1) * 128], idn_s[:]),
                      waits=[("V", M[("Ymerge", i)])] if j == 0 else (),
                      mark=("YT_t", i) if j == 7 else None)
                S(lambda e, pMb=pMb: e.activation(yt_s[:], pMb[:, 0:D],
                                                  AF.Copy),
                  waits=[("T", M[("YT_t", i)])], mark=("YTevac", i),
                  drain=False)
                for kt in range(8):
                    for half in range(2):
                        T(lambda e, kt=kt, half=half: e.matmul(
                            pM[:, half * DK:(half + 1) * DK],
                            yt_s[:, kt * 128:(kt + 1) * 128],
                            wo_s[:, kt * D + half * DK:
                                 kt * D + (half + 1) * DK],
                            start=(kt == 0), stop=(kt == 7)),
                          waits=[("S", M[("YTevac", i)])]
                          if (kt == 0 and half == 0) else (),
                          mark=("Womm", i)
                          if (kt == 7 and half == 1) else None)
                xrb = xr_s[:, sl2(i)]
                rsem = "sR" if i % 2 == 0 else "sR1"
                V(lambda e, xbuf=xbuf, xrb=xrb: e.tensor_add(
                    xrb, xbuf, pM[:]),
                  waits=[("T", M[("Womm", i)])] +
                        ([(rsem, 16 * (i // 2))] if i >= 2 else []) +
                        ([("sD", 48 * i)] if (dbg_h is not None and i >= 1)
                         else []),
                  mark=("xfree", i))
                dma("Y", rsem, lambda e, b=b, tc=tc, xrb=xrb: e.dma_start(
                    xr_h[b, tc], xrb),
                    waits=[("V", M[("xfree", i)])])
                if dbg_h is not None:
                    dma("Y", "sD", lambda e, b=b, tc=tc, xrb=xrb: e.dma_start(
                        dbg_h[b, tc], xrb))
                S(lambda e, xrb=xrb: e.activation(
                    dmy_s[:], xrb, AF.Copy,
                    accum_out=st_s[:, C_SUM2:C_SUM2 + 1]),
                  waits=[("V", M[("xfree", i)])])
                S(lambda e, xrb=xrb: e.activation(
                    dmy_s[:], xrb, AF.Square,
                    accum_out=st_s[:, C_SSQ2:C_SSQ2 + 1]),
                  mark=("ln2_stats", i), drain=False)
                V(lambda e, xrb=xrb: e.reduce_max(st_s[:, C_RMAX:C_RMAX + 1],
                                                  xrb, AXX), drain=False)
                V(lambda e, xrb=xrb: e.tensor_reduce(
                    st_s[:, C_RMIN:C_RMIN + 1], xrb, AXX, ALU.min),
                  drain=False)
                V(lambda e, i=i: e.tensor_scalar(
                    st_s[:, C_M2 + i:C_M2 + i + 1],
                    st_s[:, C_SUM2:C_SUM2 + 1], 1.0 / D, None, ALU.mult),
                  waits=[("S", M[("ln2_stats", i)])])
                V(lambda e: e.tensor_scalar(
                    st_s[:, C_U0:C_U0 + 1], st_s[:, C_SSQ2:C_SSQ2 + 1],
                    1.0 / D, None, ALU.mult), drain=False)
                V(lambda e, i=i: e.tensor_tensor(
                    st_s[:, C_U1:C_U1 + 1],
                    st_s[:, C_M2 + i:C_M2 + i + 1],
                    st_s[:, C_M2 + i:C_M2 + i + 1], ALU.mult))
                V(lambda e: e.tensor_sub(
                    st_s[:, C_U0:C_U0 + 1], st_s[:, C_U0:C_U0 + 1],
                    st_s[:, C_U1:C_U1 + 1]), mark=("ln2_var", i),
                  drain=False)
                S(lambda e, i=i: e.activation(
                    st_s[:, C_RS2 + i:C_RS2 + i + 1],
                    st_s[:, C_U0:C_U0 + 1], AF.Ln,
                    bias=st_s[:, C_E5:C_E5 + 1]),
                  waits=[("V", M[("ln2_var", i)])])
                S(lambda e, i=i: e.activation(
                    st_s[:, C_RS2 + i:C_RS2 + i + 1],
                    st_s[:, C_RS2 + i:C_RS2 + i + 1],
                    AF.Exp, scale=-0.5), mark=("ln2_rs", i), drain=False)
                V(lambda e, i=i: e.tensor_scalar(
                    st_s[:, C_U0:C_U0 + 1], st_s[:, C_RMAX:C_RMAX + 1],
                    st_s[:, C_M2 + i:C_M2 + i + 1], None, ALU.subtract),
                  waits=[("S", M[("ln2_rs", i)])], drain=False)
                V(lambda e, i=i: e.tensor_scalar(
                    st_s[:, C_U1:C_U1 + 1], st_s[:, C_RMIN:C_RMIN + 1],
                    -1.0, st_s[:, C_M2 + i:C_M2 + i + 1],
                    ALU.mult, ALU.add))
                V(lambda e: e.tensor_tensor(
                    st_s[:, C_U0:C_U0 + 1], st_s[:, C_U0:C_U0 + 1],
                    st_s[:, C_U1:C_U1 + 1], ALU.max))
                V(lambda e, i=i: e.tensor_scalar(
                    st_s[:, C_U0:C_U0 + 1], st_s[:, C_U0:C_U0 + 1],
                    st_s[:, C_RS2 + i:C_RS2 + i + 1], None, ALU.mult))
                V(lambda e: e.tensor_max(
                    st_s[:, C_VMAX:C_VMAX + 1],
                    st_s[:, C_VMAX:C_VMAX + 1],
                    st_s[:, C_U0:C_U0 + 1]), mark=("vmax", i),
                  drain=False)
            def pc_bm_Z(i):
                b, tc = i // NT, i % NT
                xbuf = xb_s[:, (i % 6) * D:(i % 6 + 1) * D]
                x1buf = x1_s[:, sl(i)]
                pQb = pQ[:].bitcast(BF16)
                pMb = pM[:].bitcast(BF16)
                xrb = xr_s[:, sl2(i)]
                rsem = "sR" if i % 2 == 0 else "sR1"
                for h in range(2):
                    Sh = (S0_s, S1_s)[h]
                    for vt in range(4):
                        pZ = (pK, pV)[vt // 2]
                        zsl = slice((vt % 2) * DK, (vt % 2 + 1) * DK)
                        T(lambda e, h=h, vt=vt, pZ=pZ, zsl=zsl: e.matmul(
                            pZ[:, zsl],
                            Vb_s[:, (i % 4) * D + h * DK + vt * 128:
                                 (i % 4) * D + h * DK + (vt + 1) * 128],
                            kht_s[:, h * DK:(h + 1) * DK],
                            start=True, stop=(tc == 0)),
                          waits=(([("S", M[("Kevac", min(i + 2, NI - 1))]),
                                   ("S", M[("Vevac", min(i + 2, NI - 1))])] +
                                  ([("V", M[("SmulH", 1, 1, i - 1)])]
                                   if i >= 1 else []))
                                 if (h == 0 and vt == 0) else []) +
                                ([("V", M[("SmulH", 0, 1, i)])]
                                 if (h == 1 and vt == 0) else []),
                          mark=("Zmm%d" % h, i)
                          if (vt == 3 and tc == 0) else None)
                        if tc > 0:
                            T(lambda e, vt=vt, pZ=pZ, zsl=zsl, Sh=Sh:
                              e.matmul(
                                  pZ[:, zsl], idn_s[:],
                                  Sh[:, vt * DK:(vt + 1) * DK],
                                  start=False, stop=True),
                              mark=("Zmm%d" % h, i) if vt == 3 else None)
                    for half in range(2):
                        pZ = (pK, pV)[half]
                        V(lambda e, h=h, half=half, pZ=pZ, Sh=Sh:
                          e.tensor_tensor(
                              Sh[:, half * 2 * DK:(half + 1) * 2 * DK]
                              .rearrange("p (r c) -> p r c", r=2),
                              pZ[:].rearrange("p (r c) -> p r c", r=2),
                              lcr_s[:, h * DK:(h + 1) * DK].unsqueeze(1)
                              .broadcast_to([128, 2, DK]), ALU.mult),
                          waits=[("T", M[("Zmm%d" % h, i)])]
                          if half == 0 else (),
                          mark=("SmulH", h, 1, i) if half == 1 else None,
                          drain=False)

            NPAIR = NI // 2
            stageA0(0)
            stageA0(1)

            wfc = {}
            tfc = {}
            W0 = NI - 8

            def wabs_chunk(p2):
                sl6 = (W0 - 2 + p2) % 6
                stg = xb_s[:, sl6 * D:(sl6 + 1) * D]
                fsem = "sF" if p2 % 2 == 0 else "sF1"
                dma("Y", fsem, lambda e, p2=p2, stg=stg: e.dma_start(
                    stg, wfT_h[p2 * 128:(p2 + 1) * 128, :]),
                    waits=[("V", M[("xfree", W0 - 2 + p2)])] +
                          ([("V", M[("wabs", p2 - 6)])] if p2 >= 6 else []))
                wfc[p2] = (fsem, dcnt[fsem])
                V(lambda e, stg=stg: e.reduce_sum(
                    st_s[:, 19:20], stg, AXX,
                    apply_absolute_value=True),
                  waits=[wfc[p2]])
                V(lambda e: e.tensor_add(
                    st_s[:, C_WABS:C_WABS + 1], st_s[:, C_WABS:C_WABS + 1],
                    st_s[:, 19:20]), mark=("wabs", p2))

            TSL = (2, 3, 0, 1)

            def tern_load(j):
                sl6 = TSL[j % 4]
                stg = xb_s[:, sl6 * D:(sl6 + 1) * D]
                fsem = "sX%d" % sl6 if sl6 else "sX0"
                if j < 2:
                    w = [("V", M[("wabs", 4 + j)])]
                elif j < 4:
                    w = [("V", M[("xfree", 28 + j)])]
                else:
                    w = [("V", M[("tern", j - 4)])]
                dma("Y", fsem, lambda e, j=j, stg=stg: e.dma_start(
                    stg, wfT_h[j * 128:(j + 1) * 128, :]), waits=w)
                tfc[j] = (fsem, dcnt[fsem])

            def halfstep(c, a, bp):
                if c is not None:
                    pc_x1tr(c)
                if a is not None:
                    pc_bm_pre(a)
                    pc_bm_lc(a)
                if c is not None and c + 2 < NI:
                    stageA0(c + 2)
                if bp is not None:
                    pc_norm(bp)
                if c is not None:
                    pc_kv(c)
                if a is not None:
                    pc_bm_A(a)
                if bp is not None:
                    stageBp(bp)
                    stageBpost(bp)
                if c is not None:
                    pc_qt(c)
                if a is not None:
                    pc_bm_Y(a)
                    pc_bm_postA(a)
                    pc_bm_Z(a)
                if a is not None:
                    pc_bm_post(a)
                if c is not None:
                    pc_vt(c)
                if c is not None and c >= W0:
                    wabs_chunk(c - W0)
                if c == NI - 2:
                    tern_load(0)
                if c == NI - 1:
                    tern_load(1)
                if c is None and a is not None and a >= NI - 2:
                    tern_load(a - (NI - 2) + 2)

            for p in range(NPAIR + 1):
                c0 = 2 * p if p < NPAIR else None
                c1 = 2 * p + 1 if p < NPAIR else None
                halfstep(c0, 2 * (p - 1) if p >= 1 else None,
                         2 * p - 1 if p >= 1 else None)
                halfstep(c1, 2 * (p - 1) + 1 if p >= 1 else None,
                         2 * p if p < NPAIR else None)

            T(lambda e: e.transpose(
                pM[0:1, 0:128], st_s[:, C_VMAX:C_VMAX + 1], idf_s[:]),
              waits=[("V", M[("vmax", NI - 1)]), ("V", M[("wabs", 7)]),
                     ("V", M[("SmulH", 1, 1, NI - 1)])])
            T(lambda e: e.transpose(
                pM[0:1, 128:256], st_s[:, C_WABS:C_WABS + 1], idf_s[:]),
              mark=("bt", 0))
            V(lambda e: e.reduce_max(st_s[0:1, C_T0:C_T0 + 1],
                                     pM[0:1, 0:128], AXX),
              waits=[("T", M[("bt", 0)])])
            V(lambda e: e.reduce_sum(st_s[0:1, C_T1:C_T1 + 1],
                                     pM[0:1, 128:256], AXX))
            V(lambda e: e.memset(t8_s[:], 1.0))
            V(lambda e: e.tensor_scalar(
                t8_s[0:1, 0:4], t8_s[0:1, 0:4], st_s[0:1, C_T0:C_T0 + 1],
                None, ALU.mult))
            V(lambda e: e.tensor_scalar(
                t8_s[0:1, 4:8], t8_s[0:1, 4:8], st_s[0:1, C_T1:C_T1 + 1],
                None, ALU.mult), mark=("t8", 0))
            dma("Y", "sC", lambda e: e.dma_start(cci_h, t8_s[:]),
                waits=[("V", M[("t8", 0)])])
            G(lambda e: e.collective_compute(
                "AllReduce", ALU.max, replica_groups=[list(range(cc_ranks))],
                ins=[cci_h], outs=[cco_h]),
              waits=[("sC", dcnt["sC"])], mark=("cc", 0))
            dma("G", "sCg", lambda e: e.dma_start(
                gbc_s[:], cco_h.partition_broadcast(128)[:, 0, :]),
                waits=[("G", M[("cc", 0)])])
            wait_only("V", [("sCg", dcnt["sCg"])])
            V(lambda e: e.tensor_copy(st_s[:, C_SC:C_SC + 1], gbc_s[:, 0:1]))
            V(lambda e: e.reciprocal(st_s[:, C_SC + 1:C_SC + 2],
                                     gbc_s[:, 0:1]))
            V(lambda e: e.tensor_scalar(
                st_s[:, C_SC + 1:C_SC + 2], st_s[:, C_SC + 1:C_SC + 2],
                127.0, None, ALU.mult))
            V(lambda e: e.tensor_scalar(
                st_s[:, C_SC + 2:C_SC + 3], gbc_s[:, 4:5],
                1.0 / (1024.0 * 1024.0), None, ALU.mult))
            V(lambda e: e.tensor_scalar(
                st_s[:, C_SC + 3:C_SC + 4], st_s[:, C_SC + 2:C_SC + 3],
                0.5, None, ALU.mult))
            V(lambda e: e.tensor_tensor(
                st_s[:, C_SC + 2:C_SC + 3], st_s[:, C_SC + 2:C_SC + 3],
                st_s[:, C_SC:C_SC + 1], ALU.mult))
            V(lambda e: e.tensor_scalar(
                st_s[:, C_SC + 2:C_SC + 3], st_s[:, C_SC + 2:C_SC + 3],
                1.0 / 127.0, None, ALU.mult), mark=("scbc", 0))
            terP = (Kb_s[:].bitcast(F32)[:, 0:D],
                    S0_s[:].bitcast(F32)[:, 0:D])
            sgnP = (QT_s[:].bitcast(F32)[:, 0:D],
                    QT_s[:].bitcast(F32)[:, D:2 * D])
            for p2 in range(8):
                if p2 >= 4:
                    tern_load(p2)
                ab, sg = terP[p2 % 2], sgnP[p2 % 2]
                sl6 = TSL[p2 % 4]
                stg = xb_s[:, sl6 * D:(sl6 + 1) * D]
                S(lambda e, stg=stg, ab=ab: e.activation(ab, stg, AF.Abs),
                  waits=[tfc[p2]] +
                        ([("V", M[("tern", p2 - 2)])] if p2 >= 2 else
                         [("V", M[("scbc", 0)])]),
                  drain=False)
                S(lambda e, stg=stg, sg=sg: e.activation(sg, stg, AF.Sign),
                  mark=("tern_s", p2), drain=False)
                V(lambda e, ab=ab: e.tensor_scalar(
                    ab, ab, st_s[:, C_SC + 3:C_SC + 4], None,
                    ALU.is_gt),
                  waits=[("S", M[("tern_s", p2)])])
                V(lambda e, p2=p2, ab=ab, sg=sg: e.tensor_tensor(
                    wf_s[:, p2 * D:(p2 + 1) * D], sg, ab,
                    ALU.mult), mark=("tern", p2), drain=False)

            C_RSIS = 96
            V(lambda e: e.tensor_scalar(
                st_s[:, C_RSIS:C_RSIS + 32], st_s[:, C_RS2:C_RS2 + 32],
                st_s[:, C_SC + 1:C_SC + 2], None, ALU.mult))
            V(lambda e: e.scalar_tensor_tensor(
                st_s[:, C_M2:C_M2 + 32], st_s[:, C_M2:C_M2 + 32], -1.0,
                st_s[:, C_RSIS:C_RSIS + 32], ALU.mult, ALU.mult))
            V(lambda e: e.reciprocal(st_s[:, C_T0:C_T0 + 1],
                                     st_s[:, C_SC + 2:C_SC + 3]))
            V(lambda e: e.tensor_scalar(
                vsc_s[:], bias_s[:], st_s[:, C_T0:C_T0 + 1], None, ALU.mult))
            V(lambda e: e.memset(dmy_s[0:1, 0:128], 1.0),
              waits=[("S", M[("tern_s", 7)])], mark=("p2prep", 0))

            KbF = Kb_s[:].bitcast(F32)
            S0F = S0_s[:].bitcast(F32)
            QTF = QT_s[:].bitcast(F32)
            x2nP = (KbF[:, 0:D], S0F[:, 0:D])
            i1P = (aarg_s[:].bitcast(I32), larg_s[:].bitcast(I32))
            uP = (aarg_s[:], larg_s[:])
            cP = (wb_s[:, 0:D], wb_s[:, D:2 * D])
            qinP = (Vb_s[:, 0:D], Vb_s[:, 2 * D:3 * D])
            qTP = (Vb_s[:, D:2 * D], Vb_s[:, 3 * D:4 * D])
            o1P = (QTF[:, 0:D], QTF[:, D:2 * D])
            pTP = (pM, pQ)
            pFP = (pK, pV)
            lqs = (("Y", "sL"), ("S", "sL1"), ("G", "sCg"))
            oqs = (("G", "sWg"), ("Y", "sO"), ("S", "sD"))
            p2lc = {}
            p2sc = {}
            sd0 = dcnt["sD"]

            def p2load(i):
                if i >= NI:
                    return
                b, tc = divmod(i, NT)
                q, sem = lqs[i % 3]
                xbuf = xb_s[:, (i % 3) * D:(i % 3 + 1) * D]
                dma(q, sem, lambda e, b=b, tc=tc, xbuf=xbuf:
                    e.dma_start(xbuf, xr_h[b, tc]),
                    waits=([("V", M[("p2free", i - 3)])] if i >= 3 else
                           [("V", M[("tern", 7)]), ("sR", dcnt["sR"]),
                            ("sR1", dcnt["sR1"])]))
                p2lc[i] = (sem, dcnt[sem])

            def p2store(i):
                b, tc = divmod(i, NT)
                q, sem = oqs[i % 3]
                o1b = o1P[i % 2]
                dma(q, sem, lambda e, b=b, tc=tc, o1b=o1b: e.dma_start(
                    out_h[b, tc], o1b),
                    waits=[("V", M[("p2free", i)])] +
                          ([("sD", sd0)] if (i % 3 == 2 and i < 5)
                           else []))
                p2sc[i] = (sem, dcnt[sem])

            def p2chainA(i):
                if i >= NI:
                    return
                p = i % 2
                xbuf = xb_s[:, (i % 3) * D:(i % 3 + 1) * D]
                x2n, i1b = x2nP[p], i1P[p]
                V(lambda e, i=i, xbuf=xbuf, x2n=x2n: e.tensor_scalar(
                    x2n, xbuf, st_s[:, C_RSIS + i:C_RSIS + i + 1],
                    st_s[:, C_M2 + i:C_M2 + i + 1], ALU.mult, ALU.add),
                  waits=[p2lc[i]] + ([("T", M[("bt", 0)])] if i < 2 else []),
                  mark=("x2n", i), drain=False)
                S(lambda e, x2n=x2n, i1b=i1b: e.activation(i1b, x2n, AF.Copy),
                  waits=[("V", M[("x2n", i)])], mark=("i1", i), drain=False)

            def p2chainB(i):
                if i >= NI:
                    return
                p = i % 2
                x2n, i1b, cb, qin = x2nP[p], i1P[p], cP[p], qinP[p]
                V(lambda e, x2n=x2n, i1b=i1b, cb=cb: e.scalar_tensor_tensor(
                    cb, i1b, 0.5, x2n, ALU.add, ALU.is_le),
                  waits=[("S", M[("i1", i)])], drain=False)
                V(lambda e, x2n=x2n, i1b=i1b: e.scalar_tensor_tensor(
                    x2n, i1b, -0.5, x2n, ALU.add, ALU.is_ge))
                V(lambda e, i1b=i1b, cb=cb, ub=uP[p]: e.tensor_tensor(
                    ub, i1b, cb, ALU.add))
                V(lambda e, qin=qin, x2n=x2n, ub=uP[p]: e.scalar_tensor_tensor(
                    qin, x2n, -1.0, ub, ALU.mult, ALU.add),
                  waits=([("T", M[("qT_t", i - 2)])] if i >= 2 else
                         [("T", M[("bt", 0)])]),
                  mark=("qin", i), drain=False)

            def p2pe_a(i):
                p = i % 2
                pT, pF = pTP[p], pFP[p]
                pTb = pT[:].bitcast(BF16)
                qin, qT = qinP[p], qTP[p]
                for j in range(8):
                    T(lambda e, j=j, pTb=pTb, qin=qin: e.transpose(
                        pTb[:, j * 128:(j + 1) * 128],
                        qin[:, j * 128:(j + 1) * 128], idn_s[:]),
                      waits=([("V", M[("qin", i)])] +
                             ([("S", M[("qT", i - 2)])] if i >= 2 else
                              ([("S", M[("QTevac", NI - 1)])] if i == 1
                               else []))) if j == 0 else (),
                      mark=("qT_t", i) if j == 7 else None)
                S(lambda e, qT=qT, pTb=pTb: e.activation(
                    qT, pTb[:, 0:D], AF.Copy),
                  waits=[("T", M[("qT_t", i)])] +
                        ([("T", M[("ffmm", i - 2)])] if i >= 2 else []),
                  mark=("qT", i), drain=False)
                for half in range(2):
                    T(lambda e, half=half, pF=pF: e.matmul(
                        pF[:, half * DK:(half + 1) * DK],
                        dmy_s[0:1, 0:128],
                        vsc_s[0:1, half * DK:(half + 1) * DK],
                        start=True, stop=False),
                      waits=(([("V", M[("p2free", i - 2)])] if i >= 2 else
                              [("V", M[("p2prep", 0)])])
                             if half == 0 else ()))
                for kt in range(8):
                    for half in range(2):
                        T(lambda e, kt=kt, half=half, pF=pF, qT=qT: e.matmul(
                            pF[:, half * DK:(half + 1) * DK],
                            qT[:, kt * 128:(kt + 1) * 128],
                            wf_s[:, kt * D + half * DK:
                                 kt * D + (half + 1) * DK],
                            start=False, stop=(kt == 7)),
                          waits=[("S", M[("qT", i)])]
                          if (kt == 0 and half == 0) else (),
                          mark=("ffmm", i) if (kt == 7 and half == 1)
                          else None)

            def p2pe_b(i):
                pF = pFP[i % 2]
                o1b = o1P[i % 2]
                xbuf = xb_s[:, (i % 3) * D:(i % 3 + 1) * D]
                V(lambda e, o1b=o1b, pF=pF, xbuf=xbuf: e.scalar_tensor_tensor(
                    o1b, pF[:], st_s[:, C_SC + 2:C_SC + 3],
                    xbuf, ALU.mult, ALU.add),
                  waits=[("T", M[("ffmm", i)])] +
                        ([p2sc[i - 2]] if i >= 2 else []),
                  mark=("p2free", i), drain=False)

            p2load(0)
            p2load(1)
            p2load(2)
            p2chainA(0)
            p2chainB(0)
            p2chainA(1)
            p2chainB(1)
            for i in range(NI):
                p2pe_a(i)
                p2chainA(i + 2)
                p2pe_b(i)
                p2chainB(i + 2)
                p2load(i + 3)
                p2store(i)
            wait_only("Y", [("sO", dcnt["sO"]), ("sWg", dcnt["sWg"]),
                            ("sD", dcnt["sD"])])

        @blk.sync
        def _(eng):
            program("Y", eng)

        @blk.gpsimd
        def _(eng):
            program("G", eng)

        @blk.scalar
        def _(eng):
            program("S", eng)

        @blk.vector
        def _(eng):
            program("V", eng)

        @blk.tensor
        def _(eng):
            program("T", eng)

    es.close()
    return nc


_NC_CACHE = {}


def kernel(L0_attn_x, L0_attn_Wq, L0_attn_Wk, L0_attn_Wv, L0_attn_Wo,
           L0_attn_W_gamma, L0_attn_b_gamma, L0_attn_W_lambda,
           L0_attn_b_lambda, L0_ffn_weight, L0_ffn_bias):
    x = np.ascontiguousarray(np.asarray(L0_attn_x, dtype=np.float32))
    assert x.shape == (16, 2048, 1024)

    def t(w):
        return np.ascontiguousarray(np.asarray(w, dtype=np.float32).T)

    wqT, wkT = t(L0_attn_Wq), t(L0_attn_Wk)
    wvT, woT = t(L0_attn_Wv), t(L0_attn_Wo)
    wfT = t(L0_ffn_weight)

    def rep(v):
        return np.ascontiguousarray(np.broadcast_to(
            np.asarray(v, np.float32).reshape(1, D), (128, D)))

    wgr, bgr = rep(L0_attn_W_gamma), rep(L0_attn_b_gamma)
    wlr, blr = rep(L0_attn_W_lambda), rep(L0_attn_b_lambda)
    biasr = rep(L0_ffn_bias)
    jj, tt = np.meshgrid(np.arange(128), np.arange(128), indexing="ij")
    tri = (jj <= tt).astype(BF)
    idn = np.eye(128, dtype=BF)
    sel = np.ones((128, 128), dtype=BF)

    if "nc" not in _NC_CACHE:
        _NC_CACHE["nc"] = build_nc()
    nc = _NC_CACHE["nc"]

    in_maps = []
    for c in range(NCORE):
        xs = np.ascontiguousarray(
            x[c * NB:(c + 1) * NB].reshape(NB, NT, TT, D))
        in_maps.append({
            "x": xs, "wqT": wqT, "wkT": wkT, "wvT": wvT, "woT": woT,
            "wfT": wfT, "wgr": wgr, "wlr": wlr, "bgr": bgr, "blr": blr,
            "biasr": biasr, "tri": tri, "idn": idn, "sel127": sel,
            "idnf": np.eye(128, dtype=np.float32),
        })
    res = run_bass_kernel_spmd(nc, in_maps, core_ids=list(range(NCORE)))
    out = np.empty((16, 2048, 1024), np.float32)
    for c in range(NCORE):
        out[c * NB:(c + 1) * NB] = np.asarray(
            res.results[c]["out"], np.float32).reshape(NB, 2048, D)
    return out
